# version 12
# baseline (speedup 1.0000x reference)
import sys

for _p in ("/opt/trn_rl_repo",):
    if _p not in sys.path:
        sys.path.insert(0, _p)

import numpy as np
import ml_dtypes
from contextlib import ExitStack

import concourse.bass as bass
import concourse.mybir as mybir
from concourse.bass_utils import run_bass_kernel_spmd

F32 = mybir.dt.float32
BF16 = mybir.dt.bfloat16
FP16 = mybir.dt.float16
I32 = mybir.dt.int32
AF = mybir.ActivationFunctionType
ALU = mybir.AluOpType
AX = mybir.AxisListType

HID = 2048
H, D, QC = 16, 128, 512
WIN = 512
G, INTER = 4, 2048
DFF = 8192
NTOK = 1024
NKV = 1536
NT = 8
NKT = 12
T_SEQ = 2048
QSC = float(1.0 / np.sqrt(np.float32(D)))
RBIAS = 256.5
BFD = ml_dtypes.bfloat16

ENGS = ("sync", "gpsimd", "tensor", "vector", "scalar")


class Ctr:
    def __init__(self):
        self.v = 0


class Phase:

    def __init__(self, ctx):
        self.ctx = ctx
        self.steps = []

    def add(self, eng, fn, deps=(), dmas=0):
        self.steps.append((eng, fn, [d for d in deps if d is not None], dmas))
        return len(self.steps) - 1

    def emit(self, nc):
        ctx = self.ctx
        sem, ctr, dsem, dctr = ctx["sem"], ctx["ctr"], ctx["dsem"], ctx["dctr"]
        NPOOL = 8
        stepval, dmaval = {}, {}
        c_eng = {e: ctr[e].v for e in ENGS}
        c_dma = {e: dctr[e].v for e in dctr}
        for i, (e, fn, deps, dmas) in enumerate(self.steps):
            c_eng[e] += 1
            stepval[i] = c_eng[e]
            if dmas:
                ws = {}
                for k in range(dmas):
                    idx = c_dma[e] + k
                    r = idx % NPOOL
                    ws[r] = 16 * (idx // NPOOL + 1)
                c_dma[e] += dmas
                dmaval[i] = (e, ws)
        present = [e for e in ENGS if any(s[0] == e for s in self.steps)]
        base = {e: ctr[e].v for e in ENGS}
        from concourse.bass_core_rust import BassInstruction as _BI
        lastins = [None]

        def _cb(ins, *a, **kw):
            lastins[0] = ins
        nc.push_inst_callback(_cb)
        try:
            with nc.Block() as blk:
                for ename in present:
                    def make(ename):
                        def body(eng):
                            for oe in ENGS:
                                if oe != ename and base[oe] > 0:
                                    eng.wait_ge(sem[oe], base[oe])
                            for i, (e, fn, deps, dmas) in enumerate(self.steps):
                                if e != ename:
                                    continue
                                waits = {}
                                for d in deps:
                                    de, _, _, ddma = self.steps[d]
                                    if ddma:
                                        qe, ws = dmaval[d]
                                        for r, v in ws.items():
                                            k = ("D", qe, r)
                                            waits[k] = max(waits.get(k, 0), v)
                                    elif de != ename:
                                        k = ("S", de, 0)
                                        waits[k] = max(waits.get(k, 0), stepval[d])
                                if ename in ("vector", "scalar") and stepval[i] - 1 > 0:
                                    eng.wait_ge(sem[ename], stepval[i] - 1)
                                for (kind, de, r), v in sorted(waits.items()):
                                    eng.wait_ge(dsem[de][r] if kind == "D" else sem[de], v)
                                if dmas:
                                    n0 = dctr[e].v
                                lastins[0] = None
                                fn(eng)
                                if dmas:
                                    assert dctr[e].v - n0 == dmas, f"step {i} dmas"
                                    eng.sem_inc(sem[ename], 1)
                                elif ename in ("tensor", "vector", "scalar") and lastins[0] is not None:
                                    _BI(lastins[0]).then_inc(sem[ename], 1)
                                else:
                                    eng.sem_inc(sem[ename], 1)
                                ctr[ename].v += 1
                        return body
                    getattr(blk, ename)(make(ename))
        finally:
            nc.remove_inst_callback(_cb)
        for e in ENGS:
            assert ctr[e].v == c_eng[e], (e, ctr[e].v, c_eng[e])


DBG_D = None
DBG_SB = 99
DBG_SA = 99


def build(upto=99, dbg=()):
    dqn, dtl, dow = DBG_D if DBG_D else (4, NT, True)
    nc = bass.Bass("TRN2", target_bir_lowering=False, detect_race_conditions=False)

    hkv = nc.dram_tensor("hkv", [NKV, HID], F32, kind="ExternalInput")
    wqc = nc.dram_tensor("wqc", [HID, QC], F32, kind="ExternalInput")
    wqu = nc.dram_tensor("wqu", [QC, H * D], F32, kind="ExternalInput")
    wk = nc.dram_tensor("wk", [HID, D], F32, kind="ExternalInput")
    wv = nc.dram_tensor("wv", [HID, D], F32, kind="ExternalInput")
    ow1 = nc.dram_tensor("ow1", [G, QC, INTER], F32, kind="ExternalInput")
    ow2 = nc.dram_tensor("ow2", [DFF, HID], F32, kind="ExternalInput")
    w1p = nc.dram_tensor("w1p", [128, 64, 2048], F32, kind="ExternalInput")
    w2p = nc.dram_tensor("w2p", [128, 4, 64, 512], F32, kind="ExternalInput")
    qwr = nc.dram_tensor("qwr", [128, 4 * D], F32, kind="ExternalInput")
    kwr = nc.dram_tensor("kwr", [128, D], F32, kind="ExternalInput")
    sinkr = nc.dram_tensor("sinkr", [128, H], F32, kind="ExternalInput")
    nsinkr = nc.dram_tensor("nsinkr", [128, H], F32, kind="ExternalInput")
    cosq = nc.dram_tensor("cosq", [NTOK, 4, 32], F32, kind="ExternalInput")
    sinq = nc.dram_tensor("sinq", [NTOK, 4, 32], F32, kind="ExternalInput")
    cosk = nc.dram_tensor("cosk", [NKV, 32], F32, kind="ExternalInput")
    sinkt = nc.dram_tensor("sinkt", [NKV, 32], F32, kind="ExternalInput")
    mtm = nc.dram_tensor("mtm", [NT, 128, 640], BF16, kind="ExternalInput")
    mdt = nc.dram_tensor("mdt", [NT, 128, 5, 128], BF16, kind="ExternalInput")
    idb = nc.dram_tensor("idb", [128, 128], BF16, kind="ExternalInput")
    idf = nc.dram_tensor("idf", [128, 128], F32, kind="ExternalInput")
    idh = nc.dram_tensor("idh", [128, 128], FP16, kind="ExternalInput")
    ones1f = nc.dram_tensor("ones1f", [1, 128], F32, kind="ExternalInput")
    onescolh = nc.dram_tensor("onescolh", [128, 1], FP16, kind="ExternalInput")
    onescolf = nc.dram_tensor("onescolf", [128, 1], F32, kind="ExternalInput")
    cvals = nc.dram_tensor("cvals", [128, 5], F32, kind="ExternalInput")

    y = nc.dram_tensor("y", [NTOK, HID], F32, kind="ExternalOutput")
    xq1d = nc.dram_tensor("xq1d", [NT, 128, 16, 128], BF16, kind="Internal")
    a8d = nc.dram_tensor("a8d", [16, 128, NTOK], BF16, kind="Internal")
    h1gd = nc.dram_tensor("h1gd", [64, 128, NTOK], FP16, kind="Internal")
    xq2d = nc.dram_tensor("xq2d", [64, 128, NTOK], FP16, kind="Internal")

    dbg_t = {name: nc.dram_tensor(name, list(shape), dt, kind="ExternalOutput")
             for name, shape, dt in dbg}

    stP = ExitStack()
    en = stP.enter_context
    with stP:
        sqwr = en(nc.sbuf_tensor("sqwr", [128, 4 * D], F32))
        skwr = en(nc.sbuf_tensor("skwr", [128, D], F32))
        ssinkc = en(nc.sbuf_tensor("ssinkc", [128, H], F32))
        snsink = en(nc.sbuf_tensor("snsink", [128, H], F32))
        ssinkexp = en(nc.sbuf_tensor("ssinkexp", [128, H], F32))
        sidb = en(nc.sbuf_tensor("sidb", [128, 128], BF16))
        sidf = en(nc.sbuf_tensor("sidf", [128, 128], F32))
        sidh = en(nc.sbuf_tensor("sidh", [128, 128], FP16))
        sones1f = en(nc.sbuf_tensor("sones1f", [1, 128], F32))
        sonescolh = en(nc.sbuf_tensor("sonescolh", [128, 1], FP16))
        sonescolf = en(nc.sbuf_tensor("sonescolf", [128, 1], F32))
        scvals = en(nc.sbuf_tensor("scvals", [128, 5], F32))
        ld0 = en(nc.sbuf_tensor("ld0", [128, 2, HID], F32))
        scr0 = en(nc.sbuf_tensor("scr0", [128, 1024], F32))
        scr1 = en(nc.sbuf_tensor("scr1", [128, HID], F32))
        scr2 = en(nc.sbuf_tensor("scr2", [128, 1024], F32))
        scri = en(nc.sbuf_tensor("scri", [128, 2, 1024], I32))
        scrbq = en(nc.sbuf_tensor("scrbq", [128, 2, HID], BF16))
        jnkA = en(nc.sbuf_tensor("jnkA", [128, 1024], F32))
        jnkB = en(nc.sbuf_tensor("jnkB", [128, HID], F32))
        kout = en(nc.sbuf_tensor("kout", [128, 128], BF16))
        tiny = en(nc.sbuf_tensor("tiny", [128, 64], F32))
        s1c = en(nc.sbuf_tensor("s1c", [128, NT], F32))
        rsx = en(nc.sbuf_tensor("rsx", [128, NT], F32))
        ssqw = en(nc.sbuf_tensor("ssqw", [128, 256], F32))
        ssqw2 = en(nc.sbuf_tensor("ssqw2", [128, 256], F32))
        tauc = en(nc.sbuf_tensor("tauc", [128, 8], F32))
        zrow = en(nc.sbuf_tensor("zrow", [16, 128], F32))
        zrowf = en(nc.sbuf_tensor("zrowf", [1, 2048], F32))
        s2c = en(nc.sbuf_tensor("s2c", [128, NT], F32))
        inv2c = en(nc.sbuf_tensor("inv2c", [128, NT], F32))
        bias2c = en(nc.sbuf_tensor("bias2c", [128, NT], F32))
        v2s = en(nc.sbuf_tensor("v2s", [128, NT], F32))
        absacc = en(nc.sbuf_tensor("absacc", [128, NTOK], F32, side="right"))
        invrep = en(nc.sbuf_tensor("invrep", [128, NTOK], F32, side="right"))
        r2rep = en(nc.sbuf_tensor("r2rep", [128, NTOK], F32, side="right"))

        sem = {e: en(nc.semaphore("s_" + e)) for e in ENGS}
        NPOOL = 8
        dsem = {q: [en(nc.semaphore(f"d_{q}{r}")) for r in range(NPOOL)]
                for q in ("sync", "gpsimd", "scalar")}
        gsem = {q: en(nc.semaphore(f"g_{q}")) for q in ("sync", "gpsimd", "scalar")}
        ssem = {e: en(nc.semaphore(f"ss_{e}")) for e in ("vector", "scalar")}
        ssc = {e: Ctr() for e in ("vector", "scalar")}

        def SV(v, inst):
            ssc["vector"].v += 1
            inst.then_inc(ssem["vector"], 1)
            v.wait_ge(ssem["vector"], ssc["vector"].v)

        def SC(s, inst):
            ssc["scalar"].v += 1
            inst.then_inc(ssem["scalar"], 1)
            s.wait_ge(ssem["scalar"], ssc["scalar"].v)
        ctr = {e: Ctr() for e in ENGS}
        dctr = {"sync": Ctr(), "gpsimd": Ctr(), "scalar": Ctr()}
        gctr = {"sync": Ctr(), "gpsimd": Ctr(), "scalar": Ctr()}
        cO = Ctr()
        ctx = dict(sem=sem, ctr=ctr, dsem=dsem, dctr=dctr)

        def _dma(eng, q, out, in_):
            idx = dctr[q].v
            r = idx % NPOOL
            if idx >= NPOOL:
                eng.wait_ge(dsem[q][r], 16 * (idx // NPOOL))
            eng.sem_inc(gsem[q], 1)
            gctr[q].v += 1
            inst = eng.dma_start(out, in_)
            inst._wait_ge(gsem[q], gctr[q].v)
            inst.then_inc(dsem[q][r], 16)
            dctr[q].v += 1

        def DL(s, out, in_):
            _dma(s, "sync", out, in_)

        def DG(g, out, in_):
            _dma(g, "gpsimd", out, in_)

        def DA(s, out, in_):
            _dma(s, "scalar", out, in_)

        def DO(s, out, in_):
            _dma(s, "sync", out, in_)
            cO.v += 1

        TC = {k: i for i, k in enumerate(
            ["hsum", "ssq", "m", "m2", "var", "rstd", "mb", "negmb", "am",
             "amx", "rs1", "t1", "bb", "kssq", "krstd", "S1", "S2",
             "rms1", "rms2", "qs0", "qs1", "qs2", "qs3", "nm", "den",
             "sinkp", "den2", "lnd", "x0", "x1", "x2", "x3", "x4", "x5",
             "dn0", "dn1", "dn2", "dn3", "z0", "z1", "z2", "z3"])}

        def tc(n):
            return tiny[:, TC[n]:TC[n] + 1]

        dump_specs = []

        def mark(name, apf):
            if name in dbg_t:
                dump_specs.append((name, apf))

        ph = Phase(ctx)

        def ld_consts(s):
            for dst, src in [
                (sqwr[:], qwr[:, :]), (skwr[:], kwr[:, :]),
                (ssinkc[:], sinkr[:, :]), (snsink[:], nsinkr[:, :]),
                (sidb[:], idb[:, :]), (sidf[:], idf[:, :]), (sidh[:], idh[:, :]),
                (sones1f[:], ones1f[:, :]), (sonescolh[:], onescolh[:, :]),
                (sonescolf[:], onescolf[:, :]), (scvals[:], cvals[:, :]),
            ]:
                DL(s, dst, src)
        c0 = ph.add("sync", ld_consts, dmas=11)
        ph.add("sync", (lambda s: None), deps=[c0])
        ph.emit(nc)

        st1 = ExitStack()
        qmidT = st1.enter_context(nc.sbuf_tensor("qmidT", [128, 4, NTOK], BF16))
        kTs = st1.enter_context(nc.sbuf_tensor("kTs", [128, NKT, 128], BF16))
        vsb = st1.enter_context(nc.sbuf_tensor("vsb", [128, NKT, 128], BF16))

        stA = ExitStack()
        enA = stA.enter_context
        swqc = enA(nc.sbuf_tensor("swqc", [128, 16, QC], BF16))
        swkv = enA(nc.sbuf_tensor("swkv", [128, 16, 256], BF16))
        xnT = enA(nc.sbuf_tensor("xnT", [128, 16, NKV], BF16))
        scrb = enA(nc.sbuf_tensor("scrb", [128, 2, HID], BF16))
        xqst = enA(nc.sbuf_tensor("xqst", [128, 2, 16, 128], BF16))
        scosk = enA(nc.sbuf_tensor("scosk", [128, NKT, 32], F32))
        ssinkt = enA(nc.sbuf_tensor("ssinkt", [128, NKT, 32], F32))

        ph = Phase(ctx)

        def ld_ropek(s):
            for j in range(NKT):
                DL(s, scosk[:, j, :], cosk[j * 128:(j + 1) * 128, :])
                DL(s, ssinkt[:, j, :], sinkt[j * 128:(j + 1) * 128, :])
        ck = ph.add("sync", ld_ropek, dmas=2 * NKT)
        ph.add("sync", (lambda s: None), deps=[ck])
        ph.add("scalar", (lambda s: s.activation(ssinkexp[:], ssinkc[:], AF.Exp)))
        wplan = []
        for i in range(16):
            wplan.append((swqc[:, i, :], wqc[i * 128:(i + 1) * 128, :], QC, None, None))
        for i in range(16):
            wplan.append((swkv[:, i, 0:128], wk[i * 128:(i + 1) * 128, :], D,
                          swkv[:, i, 128:256], wv[i * 128:(i + 1) * 128, :]))
        prevc = {}
        for i, (dst, src, ncol, dst2, src2) in enumerate(wplan):
            nd = 1 if dst2 is None else 2

            def ldw(s, i=i, src=src, ncol=ncol, src2=src2, nd=nd):
                DL(s, ld0[:, i % 2, 0:ncol], src)
                if nd == 2:
                    DL(s, ld0[:, i % 2, ncol:2 * ncol], src2)
            lid = ph.add("sync", ldw, deps=[prevc.get(i - 2)], dmas=nd)

            def cst(v, i=i, dst=dst, ncol=ncol, dst2=dst2):
                v.tensor_copy(dst, ld0[:, i % 2, 0:ncol])
                if dst2 is not None:
                    v.tensor_copy(dst2, ld0[:, i % 2, ncol:2 * ncol])
            prevc[i] = ph.add("vector", cst, deps=[lid])
        ph.emit(nc)

        stats_src = []
        for n in range(256):
            stats_src.append((w1p[:, n // 4, (n % 4) * 512:(n % 4 + 1) * 512], ssqw, n))
        for n in range(256):
            stats_src.append((w2p[:, n // 64, n % 64, :], ssqw2, n))
        nst = [0]
        sdep = {}

        def stats_slot(n):
            return ld0[:, n % 2, 1024 + ((n // 2) % 2) * 512:1536 + ((n // 2) % 2) * 512]

        spend = []

        def _stats_sq(ph_):
            m, lstep = spend.pop(0)
            srcap, out, col = stats_src[m]
            if m % 2 == 0:
                st = ph_.add("scalar", (lambda s, m=m, out=out, col=col: s.activation(
                    stats_slot(m), stats_slot(m), AF.Square,
                    accum_out=out[:, col:col + 1])), deps=[lstep])
            else:
                st = ph_.add("vector", (lambda v, m=m, out=out, col=col: v.scalar_tensor_tensor(
                    scr0[:, 0:512], stats_slot(m), 1.0, stats_slot(m), ALU.mult, ALU.mult,
                    accum_out=out[:, col:col + 1])), deps=[lstep])
            sdep[("sq", m, "ph")] = ph_
            sdep[("sq", m)] = st

        def emit_stats(ph_, k):
            if upto < 3:
                return
            for _ in range(k):
                n = nst[0]
                if n >= 512:
                    return
                if len(spend) >= 3:
                    _stats_sq(ph_)
                srcap, out, col = stats_src[n]
                wdep = sdep.get(("sq", n - 4)) if sdep.get(("sq", n - 4, "ph")) is ph_ else None
                lstep = ph_.add("scalar", (lambda s, n=n, srcap=srcap: DA(
                    s, stats_slot(n), srcap)),
                    deps=[wdep], dmas=1)
                spend.append((n, lstep))
                nst[0] += 1

        def emit_stats_flush(ph_):
            while spend:
                _stats_sq(ph_)

        NKT_B = 0 if upto < 1 else NKT
        with ExitStack() as pst:
            ptr = pst.enter_context(nc.psum_tensor("ptrB", [128, 1024], BF16))
            ptr2 = pst.enter_context(nc.psum_tensor("ptrB2", [128, 1024], BF16))
            pq = pst.enter_context(nc.psum_tensor("pqC", [128, 1024], F32))
            pkv = [pst.enter_context(nc.psum_tensor(f"pkvC{i}", [128, 256], F32)) for i in range(2)]
            ptrc = pst.enter_context(nc.psum_tensor("ptrC", [128, 128], BF16))
            ph = Phase(ctx)
            pv = {}

            for j in range(NKT_B):
                jo = j - 4
                Lj = ph.add("sync", (lambda s, j=j: DL(s, ld0[:, j % 2, :], hkv[j * 128:(j + 1) * 128, :])),
                            deps=[pv.get(("fl", j - 2))], dmas=1)
                hb = ld0[:, j % 2, :]

                def c1(s, hb=hb, j=j):
                    s.activation(jnkB[:], hb, AF.Square, accum_out=tc("ssq"))
                    s.activation(jnkB[:], hb, AF.Copy, accum_out=tc("hsum"))
                C1 = ph.add("scalar", c1, deps=[Lj])

                def v1(v):
                    SV(v, v.tensor_scalar(tc("m"), tc("hsum"), float(1.0 / HID), None, ALU.mult))
                    SV(v, v.tensor_mul(tc("m2"), tc("m"), tc("m")))
                    v.scalar_tensor_tensor(tc("var"), tc("ssq"), float(1.0 / HID), tc("m2"), ALU.mult, ALU.subtract)
                V1 = ph.add("vector", v1, deps=[C1])
                C2 = ph.add("scalar", (lambda s: s.activation(tc("x0"), tc("var"), AF.Sqrt, bias=scvals[:, 0:1])), deps=[V1])

                def v2(v):
                    SV(v, v.reciprocal(tc("rstd"), tc("x0")))
                    SV(v, v.tensor_mul(tc("mb"), tc("m"), tc("rstd")))
                    v.tensor_scalar(tc("negmb"), tc("mb"), -1.0, None, ALU.mult)
                V2 = ph.add("vector", v2, deps=[C2])
                C3 = ph.add("scalar", (lambda s, hb=hb, j=j: s.activation(
                    scrb[:, j % 2, :], hb, AF.Identity, bias=tc("negmb"), scale=tc("rstd"))),
                    deps=[V2, pv.get(("P1", j - 2))])

                def p1(t, j=j):
                    for kk in range(16):
                        pt = ptr if kk < 8 else ptr2
                        t.transpose(pt[:, (kk % 8) * 128:(kk % 8 + 1) * 128],
                                    scrb[:, j % 2, kk * 128:(kk + 1) * 128], sidb[:])
                P1 = ph.add("tensor", p1, deps=[C3, pv.get(("C4", j - 1)), pv.get(("C7", j - 1))])

                def c4(s, j=j):
                    s.copy(xnT[:, 0:8, j * 128:(j + 1) * 128], ptr.ap().rearrange("p (a b) -> p a b", a=8))
                    s.copy(xnT[:, 8:16, j * 128:(j + 1) * 128], ptr2.ap().rearrange("p (a b) -> p a b", a=8))
                C4 = ph.add("scalar", c4, deps=[P1])
                pv[("C4", j)] = C4
                pv[("P1", j)] = P1

                if jo >= 0:
                    def v3(v, j=j, jo=jo):
                        SV(v, v.tensor_reduce(tc("am"), scrb[:, j % 2, :], AX.X, ALU.max, apply_absolute_value=True))
                        SV(v, v.tensor_scalar(tc("amx"), tc("am"), 1e-5, None, ALU.max))
                        SV(v, v.reciprocal(tc("x1"), tc("amx")))
                        v.tensor_scalar(s1c[:, jo:jo + 1], tc("x1"), 127.0, None, ALU.mult)
                    C5 = V3 = ph.add("vector", v3, deps=[C3])

                    def v4(v, jo=jo):
                        v.tensor_mul(tc("rs1"), tc("rstd"), s1c[:, jo:jo + 1])
                        SV(v, v.tensor_mul(tc("t1"), tc("mb"), s1c[:, jo:jo + 1]))
                        v.tensor_scalar(tc("bb"), tc("t1"), -1.0, RBIAS, ALU.mult, ALU.add)
                    V4 = ph.add("vector", v4, deps=[C5])
                    C6 = ph.add("scalar", (lambda s, hb=hb: SC(s, s.activation(
                        scr1[:], hb, AF.Identity, bias=tc("bb"), scale=tc("rs1")))),
                        deps=[V4, pv.get(("kvf", j - 1))])

                    def v5(v, jo=jo, j=j):
                        for hf in range(2):
                            s1h = scr1[:, hf * 1024:(hf + 1) * 1024]
                            v.tensor_copy(scri[:, 0, :], s1h)
                            v.scalar_tensor_tensor(scr2[:], s1h, 1.0, scri[:, 0, :], ALU.mult, ALU.subtract)
                            v.tensor_scalar(scr0[:], scr2[:], 1.0, None, ALU.is_ge)
                            v.scalar_tensor_tensor(scr0[:], scr2[:], 0.0, scr0[:], ALU.is_ge, ALU.add)
                            v.scalar_tensor_tensor(scr2[:], scr0[:], -257.0, scri[:, 0, :], ALU.add, ALU.add)
                            v.tensor_scalar(scrbq[:, j % 2, hf * 1024:(hf + 1) * 1024], scr2[:], -128.0, 127.0, ALU.max, ALU.min)
                        v.tensor_reduce(rsx[:, jo:jo + 1], scrbq[:, j % 2, :], AX.X, ALU.add)
                    V5 = ph.add("vector", v5, deps=[C6, pv.get(("P2", j - 1))])

                    def p2(t, j=j):
                        for kk in range(16):
                            pt = ptr if kk < 8 else ptr2
                            t.transpose(pt[:, (kk % 8) * 128:(kk % 8 + 1) * 128],
                                        scrbq[:, j % 2, kk * 128:(kk + 1) * 128], sidb[:])
                    P2 = ph.add("tensor", p2, deps=[V5, C4])

                    def c7(s, jo=jo, j=j):
                        s.copy(xqst[:, j % 2, 0:8, :], ptr.ap().rearrange("p (a b) -> p a b", a=8))
                        SC(s, s.copy(xqst[:, j % 2, 8:16, :], ptr2.ap().rearrange("p (a b) -> p a b", a=8)))
                        DA(s, xq1d[jo, :, :, :], xqst[:, j % 2, :, :])
                    C7 = ph.add("scalar", c7, deps=[P2, pv.get(("C7", j - 2))], dmas=1)
                    pv[("C7", j)] = C7
                    pv[("P2", j)] = P2
                    pv[("fl", j)] = C6
                else:
                    pv[("fl", j)] = C3

                if upto >= 2:
                    def kvmm(t, j=j):
                        for kk in range(16):
                            t.matmul(pkv[j % 2][:, :], xnT[:, kk, j * 128:(j + 1) * 128],
                                     swkv[:, kk, :], start=(kk == 0), stop=(kk == 15))
                    Pm = ph.add("tensor", kvmm, deps=[pv.get(("kvf", j - 2)), C4])

                    def kstat(s, j=j):
                        SC(s, s.activation(jnkA[:, 0:128], pkv[j % 2][:, 0:128], AF.Square, accum_out=tc("kssq")))
                        s.activation(tc("x2"), tc("kssq"), AF.Sqrt, scale=float(1.0 / D), bias=scvals[:, 1:2])
                        s.copy(vsb[:, j, :], pkv[j % 2][:, 128:256])
                    Cs = ph.add("scalar", kstat, deps=[Pm])

                    def krope(v, j=j):
                        SV(v, v.reciprocal(tc("krstd"), tc("x2")))
                        p = pkv[j % 2]
                        x1, x2 = p[:, 0:32], p[:, 32:64]
                        cs, sn = scosk[:, j, :], ssinkt[:, j, :]
                        v.tensor_mul(scr0[:, 0:32], x1, cs)
                        SV(v, v.tensor_mul(scr0[:, 32:64], x2, sn))
                        v.tensor_sub(scr1[:, 0:32], scr0[:, 0:32], scr0[:, 32:64])
                        v.tensor_mul(scr0[:, 0:32], x1, sn)
                        SV(v, v.tensor_mul(scr0[:, 32:64], x2, cs))
                        SV(v, v.tensor_add(scr1[:, 32:64], scr0[:, 0:32], scr0[:, 32:64]))
                        SV(v, v.tensor_scalar_mul(scr1[:, 0:64], scr1[:, 0:64], tc("krstd")))
                        SV(v, v.tensor_scalar_mul(scr1[:, 64:128], p[:, 64:128], tc("krstd")))
                        v.tensor_mul(kout[:], scr1[:, 0:128], skwr[:])
                    Vr = ph.add("vector", krope, deps=[Cs, pv.get(("ktr", j - 1)), V5 if jo >= 0 else None])
                    Pt = ph.add("tensor", (lambda t: t.transpose(ptrc[:], kout[:], sidb[:])),
                                deps=[Vr, pv.get(("ke", j - 1))])
                    pv[("ktr", j)] = Pt
                    Ce = ph.add("scalar", (lambda s, j=j: s.copy(kTs[:, j, :], ptrc[:])), deps=[Pt])
                    pv[("ke", j)] = Ce
                    pv[("kvf", j)] = Vr

            if upto >= 2:
                for qc in range(4):
                    def pmm(t, qc=qc):
                        for kk in range(16):
                            for hf in range(2):
                                t.matmul(pq[:, hf * 512:(hf + 1) * 512],
                                         swqc[:, kk, qc * 128:(qc + 1) * 128],
                                         xnT[:, kk, 512 + hf * 512:512 + (hf + 1) * 512],
                                         start=(kk == 0), stop=(kk == 15))
                    Pm = ph.add("tensor", pmm, deps=[pv.get(("qe", qc - 1)), pv.get(("C4", NKT_B - 1))])
                    pv[("qe", qc)] = ph.add("scalar", (lambda s, qc=qc: s.copy(qmidT[:, qc, :], pq[:])), deps=[Pm])
            ph.emit(nc)
            if upto == 3:
                ph = Phase(ctx)
                emit_stats(ph, 512)
                emit_stats_flush(ph)
                ph.emit(nc)

        mark("d_xnT", lambda: xnT[:])
        mark("d_s1c", lambda: s1c[:])
        mark("d_rsx", lambda: rsx[:])
        mark("d_xq1d7", lambda: xq1d[7, :, :, :])
        mark("d_qmidT", lambda: qmidT[:])
        mark("d_kT", lambda: kTs[:])
        mark("d_v", lambda: vsb[:])

        def emit_s2():
            with ExitStack() as pst:
                p1x1 = pst.enter_context(nc.psum_tensor("p1x1", [128, 512], F32))
                prep = pst.enter_context(nc.psum_tensor("prepS", [128, 1024], F32))
                ph = Phase(ctx)

                def red(v):
                    v.tensor_reduce(tc("x0"), ssqw[:], AX.X, ALU.add)
                    v.tensor_reduce(tc("x1"), ssqw2[:], AX.X, ALU.add)
                R1 = ph.add("vector", red)

                def m1(t):
                    t.matmul(p1x1[0:1, 0:1], sonescolf[:], tc("x0"), start=True, stop=True)
                    t.matmul(p1x1[0:1, 1:2], sonescolf[:], tc("x1"), start=True, stop=True)
                M1 = ph.add("tensor", m1, deps=[R1])
                Cm = ph.add("scalar", (lambda s: s.activation(
                    tiny[0:1, TC["S1"]:TC["S1"] + 2], p1x1[0:1, 0:2], AF.Copy, scale=float(1.0 / (DFF * HID)))), deps=[M1])
                Vm = ph.add("vector", (lambda v: v.tensor_scalar(
                    tiny[0:1, TC["S1"]:TC["S1"] + 2], tiny[0:1, TC["S1"]:TC["S1"] + 2], 1e-5, None, ALU.max)), deps=[Cm])

                def acts2(s):
                    s.activation(tiny[0:1, TC["x2"]:TC["x2"] + 1], tiny[0:1, TC["S1"]:TC["S1"] + 1], AF.Sqrt, scale=0.25)
                    SC(s, s.activation(tiny[0:1, TC["x3"]:TC["x3"] + 1], tiny[0:1, TC["S2"]:TC["S2"] + 1], AF.Sqrt, scale=0.25))
                    s.activation(tiny[0:1, TC["x4"]:TC["x4"] + 1], tiny[0:1, TC["x2"]:TC["x2"] + 1], AF.Copy, scale=-1.0)
                    s.activation(tiny[0:1, TC["x5"]:TC["x5"] + 1], tiny[0:1, TC["x3"]:TC["x3"] + 1], AF.Copy, scale=-1.0)
                Ca0 = ph.add("scalar", acts2, deps=[Vm])

                def vrms(v):
                    v.reciprocal(tiny[0:1, TC["x0"]:TC["x0"] + 1], tiny[0:1, TC["x2"]:TC["x2"] + 1])
                    SV(v, v.reciprocal(tiny[0:1, TC["x1"]:TC["x1"] + 1], tiny[0:1, TC["x3"]:TC["x3"] + 1]))
                    v.tensor_scalar(tiny[0:1, TC["rms1"]:TC["rms1"] + 1], tiny[0:1, TC["x0"]:TC["x0"] + 1], 0.5, None, ALU.mult)
                    v.tensor_scalar(tiny[0:1, TC["rms2"]:TC["rms2"] + 1], tiny[0:1, TC["x1"]:TC["x1"] + 1], 0.5, None, ALU.mult)
                Ca = ph.add("vector", vrms, deps=[Ca0])

                def bcast(t):
                    for col, src in [(0, "x2"), (1, "x4"), (2, "x3"), (3, "x5"), (4, "rms1"), (5, "rms2")]:
                        t.matmul(p1x1[:, 8 + col:9 + col], sones1f[:], tiny[0:1, TC[src]:TC[src] + 1],
                                 start=True, stop=True)
                Mb = ph.add("tensor", bcast, deps=[Ca])
                Cb = ph.add("scalar", (lambda s: s.copy(tauc[:, 0:6], p1x1[:, 8:14])), deps=[Mb])
                def vinv(v):
                    SV(v, v.tensor_scalar_mul(scr0[:, 0:NT], s1c[:, 0:NT], tauc[:, 4:5]))
                    SV(v, v.reciprocal(scr1[:, 0:NT], scr0[:, 0:NT]))
                    v.tensor_mul(scr1[:, NT:2 * NT], rsx[:, 0:NT], scr1[:, 0:NT])
                Vj = ph.add("vector", vinv, deps=[Cb])
                Pk = Vj
                for bb in range(4):
                    def trb(t, bb=bb):
                        for c in range(4):
                            t.transpose(p1x1[0:1, c * 128:(c + 1) * 128], scr1[:, bb * 4 + c:bb * 4 + c + 1], sidf[:])
                    Pk2 = ph.add("tensor", trb, deps=[Pk])
                    Pk = ph.add("scalar", (lambda s, bb=bb: s.copy(zrowf[0:1, bb * 512:(bb + 1) * 512], p1x1[0:1, 0:512])),
                                deps=[Pk2])

                def rank1(t):
                    for tt in range(NT):
                        t.matmul(prep[:, tt * 128:(tt + 1) * 128], sones1f[:], zrowf[0:1, tt * 128:(tt + 1) * 128],
                                 start=True, stop=True)
                Pr = ph.add("tensor", rank1, deps=[Pk])
                Cr = ph.add("scalar", (lambda s: s.copy(invrep[:], prep[:])), deps=[Pr])

                def rank2(t):
                    for tt in range(NT):
                        t.matmul(prep[:, tt * 128:(tt + 1) * 128], sones1f[:],
                                 zrowf[0:1, 1024 + tt * 128:1024 + (tt + 1) * 128], start=True, stop=True)
                Pr2 = ph.add("tensor", rank2, deps=[Cr])
                ph.add("scalar", (lambda s: s.copy(r2rep[:], prep[:])), deps=[Pr2])
                ph.emit(nc)

        yac = None
        if upto >= 3:
            stA.close()
            yac = en(nc.sbuf_tensor("yac", [128, NT, HID], FP16, side="right"))
            if upto == 3:
                emit_s2()

            mark("d_tauc", lambda: tauc[:])
            mark("d_s2c", lambda: s2c[:])
            mark("d_inv2c", lambda: inv2c[:])
            mark("d_bias2c", lambda: bias2c[:])
            mark("d_v2s", lambda: v2s[:])
            mark("d_absacc", lambda: absacc[:])
            mark("d_h1g", lambda: h1gd[0, :, :])
            mark("d_h1gL", lambda: h1gd[63, :, :])
            mark("d_xq2", lambda: xq2d[0, :, :])
            mark("d_invrep", lambda: invrep[:])
            mark("d_r2rep", lambda: r2rep[:])

        if upto >= 4:
            stB = ExitStack()
            enB = stB.enter_context
            scosq_s = enB(nc.sbuf_tensor("scosq", [128, 2, 4, 32], F32))
            ssinq_s = enB(nc.sbuf_tensor("ssinq", [128, 2, 4, 32], F32))
            swquq = enB(nc.sbuf_tensor("swquq", [128, 4, 512], BF16))
            smtm = enB(nc.sbuf_tensor("smtm", [128, 2, 640], BF16))
            w1g = enB(nc.sbuf_tensor("w1g", [128, 4, INTER], BF16))
            aT = enB(nc.sbuf_tensor("aT", [128, 4, NTOK], BF16))
            a8st = enB(nc.sbuf_tensor("a8st", [128, 2, NTOK], BF16))
            a8s = enB(nc.sbuf_tensor("a8s", [128, 4, NTOK], BF16))
            pTb = enB(nc.sbuf_tensor("pTb", [128, 5, 512], BF16))
            qnTt = enB(nc.sbuf_tensor("qnTt", [128, 4, 128], BF16))
            scrF = enB(nc.sbuf_tensor("scrF", [128, 2, 4, 640], FP16))
            scrP = enB(nc.sbuf_tensor("scrP", [128, 2, 640], F32))
            nzh = enB(nc.sbuf_tensor("nzh", [128, 16], F32))
            ow2b = enB(nc.sbuf_tensor("ow2b", [128, 4, 512], BF16))


            qro = enB(nc.sbuf_tensor("qro", [128, 512], FP16))
            for q in range(dqn):
                with ExitStack() as pst:
                    Pq2 = [pst.enter_context(nc.psum_tensor(f"PqD{q}_{i}", [128, 512], F32)) for i in range(2)]
                    Ptm = [pst.enter_context(nc.psum_tensor(f"PtmD{q}_{i}", [128, 640], F32)) for i in range(2)]
                    Ptc = pst.enter_context(nc.psum_tensor(f"PtcD{q}", [128, 512], FP16))
                    Pa = pst.enter_context(nc.psum_tensor(f"PaD{q}", [128, 512], F32))
                    ph = Phase(ctx)
                    pv = {}
                    for i in range(4):
                        def ldow(s, i=i, q=q):
                            DL(s, ld0[:, i % 2, :], ow1[q, i * 128:(i + 1) * 128, :])
                            DL(s, jnkA[:, 0:512], wqu[i * 128:(i + 1) * 128, q * 512:(q + 1) * 512])
                        Lw = ph.add("sync", ldow, deps=[pv.get(("w1c", i - 1))], dmas=2)

                        def cstow(v, i=i):
                            v.tensor_copy(w1g[:, i, :], ld0[:, i % 2, :])
                            v.tensor_copy(swquq[:, i, :], jnkA[:, 0:512])
                        pv[("w1c", i)] = ph.add("vector", cstow, deps=[Lw])

                    def stage_b(t):
                        for c in range(5):
                            def ptr_(tn, c=c, t=t):
                                for hh in range(4):
                                    tn.transpose(Ptc[:, hh * 128:(hh + 1) * 128],
                                                 scrF[:, t % 2, hh, c * 128:(c + 1) * 128], sidh[:])
                            PT = ph.add("tensor", ptr_, deps=[pv[("zrpn", t, 1)], pv.get(("pte", t, c - 1)),
                                                              pv.get(("qtre", t + 1))])
                            pv[("pt", t, c)] = PT
                            PTE = ph.add("vector", (lambda v, c=c: v.tensor_copy(pTb[:, c, :], Ptc[:, :])),
                                         deps=[PT, pv.get(("bt", t - 1))])
                            pv[("pte", t, c)] = PTE

                        def beta(tn, t=t):
                            for c in range(5):
                                tn.matmul(Pa[:, :], vsb[:, t + c, :], pTb[:, c, :],
                                          start=(c == 0), stop=(c == 4))
                        BT = ph.add("tensor", beta, deps=[pv[("pte", t, 4)], pv.get(("be", t - 1))])
                        pv[("bt", t)] = BT
                        BE = ph.add("vector", (lambda v, t=t: v.tensor_copy(
                            aT[:, :, t * 128:(t + 1) * 128], Pa.ap().rearrange("p (a b) -> p a b", a=4))),
                            deps=[BT])
                        pv[("be", t)] = BE

                    def mk_q(t):
                        def ldm(s, t=t):
                            DL(s, smtm[:, t % 2, :], mtm[t, :, :])
                            DL(s, scosq_s[:, t % 2, :, :], cosq[t * 128:(t + 1) * 128, :, :])
                            DL(s, ssinq_s[:, t % 2, :, :], sinq[t * 128:(t + 1) * 128, :, :])
                        Lm = ph.add("sync", ldm,
                                    deps=[pv.get(("tse", t - 2, 3)), pv.get(("rope", t - 2))], dmas=3)
                        pv[("lm", t)] = Lm

                        def pqm(tn, t=t, q=q):
                            for qc in range(4):
                                tn.matmul(Pq2[t % 2][:, :], qmidT[:, qc, t * 128:(t + 1) * 128],
                                          swquq[:, qc, :],
                                          start=(qc == 0), stop=(qc == 3))
                        PQ = ph.add("tensor", pqm, deps=[pv.get(("rope", t - 2)), pv[("w1c", 3)]])

                        def qsq(s, t=t):
                            for hh in range(4):
                                ii = s.activation(jnkA[:, hh * 128:(hh + 1) * 128], Pq2[t % 2][:, hh * 128:(hh + 1) * 128],
                                                  AF.Square, accum_out=tc("qs%d" % hh))
                            SC(s, ii)
                            for hh in range(4):
                                s.activation(tc("qs%d" % hh), tc("qs%d" % hh), AF.Sqrt, bias=scvals[:, 2:3])
                        pv[("qs", t)] = ph.add("scalar", qsq, deps=[PQ, pv.get(("rope", t - 1))])

                    def mk_rope(t):
                        def rope(v, t=t):
                            for hh in range(4):
                                ii = v.reciprocal(tc("qs%d" % hh), tc("qs%d" % hh))
                            SV(v, ii)
                            pqv = Pq2[t % 2].ap().rearrange("p (h d) -> p h d", h=4)
                            s0 = scr0[:, 0:512].rearrange("p (h d) -> p h d", h=4)
                            s2 = scr2[:, 0:512].rearrange("p (h d) -> p h d", h=4)
                            x1, x2 = pqv[:, :, 0:32], pqv[:, :, 32:64]
                            cs, sn = scosq_s[:, t % 2, :, :], ssinq_s[:, t % 2, :, :]
                            v.tensor_mul(s2[:, :, 0:32], x1, cs)
                            v.tensor_mul(s2[:, :, 32:64], x2, sn)
                            v.tensor_sub(s0[:, :, 0:32], s2[:, :, 0:32], s2[:, :, 32:64])
                            v.tensor_mul(s2[:, :, 0:32], x1, sn)
                            v.tensor_mul(s2[:, :, 32:64], x2, cs)
                            v.tensor_add(s0[:, :, 32:64], s2[:, :, 0:32], s2[:, :, 32:64])
                            for hh in range(4):
                                v.tensor_scalar_mul(scr1[:, hh * 128:hh * 128 + 64],
                                                    scr0[:, hh * 128:hh * 128 + 64], tc("qs%d" % hh))
                                v.tensor_scalar_mul(scr1[:, hh * 128 + 64:hh * 128 + 128],
                                                    Pq2[t % 2][:, hh * 128 + 64:hh * 128 + 128], tc("qs%d" % hh))
                            v.tensor_mul(qro[:], scr1[:, 0:512], sqwr[:])
                        pv[("rope", t)] = ph.add("vector", rope,
                                                 deps=[pv[("qs", t)], pv.get(("qtr", t - 1)), pv[("lm", t)]])

                    mk_q(0)
                    mk_rope(0)
                    for t in range(dtl):
                        Lm = pv[("lm", t)]
                        if t + 1 < dtl:
                            mk_q(t + 1)

                        def qtr(tn, t=t):
                            for hh in range(4):
                                tn.transpose(Ptc[:, hh * 128:(hh + 1) * 128],
                                             qro[:, hh * 128:(hh + 1) * 128], sidh[:])
                        QT = ph.add("tensor", qtr, deps=[pv[("rope", t)], pv.get(("pte", t - 2, 4))])
                        pv[("qtr", t)] = QT
                        QTE = ph.add("vector", (lambda v: v.tensor_copy(qnTt[:, :, :], Ptc.ap().rearrange("p (a b) -> p a b", a=4))),
                                     deps=[QT, pv.get(("tm", t - 1, 3))])
                        pv[("qtre", t)] = QTE

                        def mk_tm(hh):
                            def tmm(tn, t=t, hh=hh):
                                tn.matmul(Ptm[hh % 2][:, 0:512], qnTt[:, hh, :], kTs[:, t:t + 4, :], start=True, stop=True)
                                tn.matmul(Ptm[hh % 2][:, 512:640], qnTt[:, hh, :], kTs[:, t + 4, :], start=True, stop=True)
                            pv[("tm", t, hh)] = ph.add("tensor", tmm,
                                                       deps=[QTE, Lm,
                                                             pv.get(("tse", t, hh - 2)) if hh >= 2
                                                             else pv.get(("tse", t - 1, hh + 2))])

                        def mk_tse(hh):
                            TSE = ph.add("vector", (lambda v, t=t, hh=hh: v.tensor_add(
                                scrF[:, t % 2, hh, :], Ptm[hh % 2][:], smtm[:, t % 2, :])),
                                deps=[pv[("tm", t, hh)], Lm, pv.get(("pt", t - 2, 4))])
                            pv[("tse", t, hh)] = TSE
                        def mk_ex1(hh):
                            dep2 = pv.get(("zrpn", t, 0)) if hh >= 2 else pv.get(("zrpn", t - 1, 1))
                            EX1 = ph.add("scalar", (lambda s, hh=hh, t=t: s.activation(
                                scrP[:, hh % 2, :], scrF[:, t % 2, hh, :], AF.Exp,
                                accum_out=tiny[:, TC["dn0"] + hh:TC["dn0"] + hh + 1])),
                                deps=[pv[("tse", t, hh)], dep2])
                            pv[("ex1", t, hh)] = EX1

                        def mk_sb(hh):
                            hg = q * 4 + hh
                            pv[("sb", t, hh)] = ph.add("scalar", (lambda s, hh=hh, hg=hg: s.activation(
                                tiny[:, TC["z0"] + hh:TC["z0"] + hh + 1],
                                tiny[:, TC["dn0"] + hh:TC["dn0"] + hh + 1],
                                AF.Identity, bias=ssinkexp[:, hg:hg + 1])),
                                deps=[pv[("ex1", t, hh)], pv.get(("zrpn", t - 1, hh // 2))])

                        def mk_zrpn(pr):
                            def zrpn(v, pr=pr, t=t):
                                SV(v, v.reciprocal(nzh[:, 2 * pr:2 * pr + 2],
                                                   tiny[:, TC["z0"] + 2 * pr:TC["z0"] + 2 * pr + 2]))
                                for hh in (2 * pr, 2 * pr + 1):
                                    v.tensor_scalar_mul(scrF[:, t % 2, hh, :], scrP[:, hh % 2, :],
                                                        nzh[:, hh:hh + 1])
                            pv[("zrpn", t, pr)] = ph.add("vector", zrpn,
                                                         deps=[pv[("sb", t, 2 * pr + 1)],
                                                               pv.get(("pt", t - 2, 4))])
                        mk_tm(0)
                        mk_tm(1)
                        mk_tse(0)
                        mk_tm(2)
                        mk_tse(1)
                        mk_tm(3)
                        mk_tse(2)
                        mk_tse(3)
                        mk_ex1(0)
                        mk_ex1(1)
                        mk_sb(0)
                        mk_sb(1)
                        mk_zrpn(0)
                        mk_ex1(2)
                        mk_ex1(3)
                        mk_sb(2)
                        mk_sb(3)
                        mk_zrpn(1)
                        if t + 1 < dtl:
                            mk_rope(t + 1)
                        if t >= 1:
                            stage_b(t - 1)
                    stage_b(dtl - 1)

                    for oc in range(16 if dow else 0):
                        def owm(tn, oc=oc):
                            for i in range(4):
                                tn.matmul(Pq2[0][:, :], w1g[:, i, oc * 128:(oc + 1) * 128], aT[:, i, 0:512],
                                          start=(i == 0), stop=(i == 3))
                            for i in range(4):
                                tn.matmul(Pq2[1][:, :], w1g[:, i, oc * 128:(oc + 1) * 128], aT[:, i, 512:1024],
                                          start=(i == 0), stop=(i == 3))
                        OM = ph.add("tensor", owm, deps=[pv[("be", NT - 1)], pv[("w1c", 3)],
                                                         pv.get(("rope", NT - 1)),
                                                         pv.get(("owe", oc - 1))])

                        def owe(s, oc=oc):
                            s.copy(a8st[:, oc % 2, 0:512], Pq2[0][:, :])
                            SC(s, s.copy(a8st[:, oc % 2, 512:1024], Pq2[1][:, :]))
                            DA(s, a8d[oc, :, :], a8st[:, oc % 2, :])
                        pv[("owe", oc)] = ph.add("scalar", owe, deps=[OM, pv.get(("owe", oc - 2))], dmas=1)
                    if dow:
                        ph.add("scalar", (lambda s: None), deps=[pv[("owe", 15)]])
                    ph.emit(nc)

                with ExitStack() as pst:
                    Pw = [pst.enter_context(nc.psum_tensor(f"PwD{q}_{i}", [128, 512], F32)) for i in range(8)]
                    ph = Phase(ctx)
                    pv = {}
                    for nq in range(4):
                        for kc in range(16):
                            Lw = ph.add("sync", (lambda s, nq=nq, kc=kc, q=q: DL(
                                s, ld0[:, kc % 2, 0:512],
                                ow2[q * 2048 + kc * 128:q * 2048 + (kc + 1) * 128, nq * 512:(nq + 1) * 512])),
                                deps=[pv.get(("cst", nq, kc - 2)), pv.get(("cst", nq - 1, 14 + kc))],
                                dmas=1)
                            La = ph.add("gpsimd", (lambda g, kc=kc: DG(g, a8s[:, kc % 4, :], a8d[kc, :, :])),
                                        deps=[pv.get(("mm", nq, kc - 4)) if kc >= 4 else pv.get(("mm", nq - 1, 12 + kc))],
                                        dmas=1)
                            CS = ph.add("vector", (lambda v, kc=kc: v.tensor_copy(
                                ow2b[:, kc % 4, :], ld0[:, kc % 2, 0:512])),
                                deps=[Lw, pv.get(("mm", nq, kc - 4)) if kc >= 4 else pv.get(("mm", nq - 1, 12 + kc))])
                            pv[("cst", nq, kc)] = CS

                            def mms(tn, nq=nq, kc=kc):
                                for t in range(NT):
                                    tn.matmul(Pw[t][:, :], a8s[:, kc % 4, t * 128:(t + 1) * 128],
                                              ow2b[:, kc % 4, :], start=(kc == 0), stop=(kc == 15))
                            pv[("mm", nq, kc)] = ph.add("tensor", mms,
                                                        deps=[CS, La, pv.get(("ev", nq - 1, NT - 1))])
                            emit_stats(ph, 2)
                        for t in range(NT):
                            if q == 0:
                                EV = ph.add("scalar", (lambda s, nq=nq, t=t: s.copy(
                                    yac[:, t, nq * 512:(nq + 1) * 512], Pw[t][:, :])), deps=[pv[("mm", nq, 15)]])
                            else:
                                EV = ph.add("vector", (lambda v, nq=nq, t=t: v.tensor_add(
                                    yac[:, t, nq * 512:(nq + 1) * 512], Pw[t][:, :],
                                    yac[:, t, nq * 512:(nq + 1) * 512])), deps=[pv[("mm", nq, 15)]])
                            pv[("ev", nq, t)] = EV
                    emit_stats_flush(ph)
                    ph.emit(nc)
            mark("d_qnTt", lambda: qnTt[:])
            emit_s2()
            mark("d_pTb", lambda: pTb[:])
            mark("d_aT", lambda: aT[:])
            stB.close()
            mark("d_yac", lambda: yac[:])

        if upto >= 5:
            st1.close()
            stF = ExitStack()
            enF = stF.enter_context
            w1Tp = enF(nc.sbuf_tensor("w1Tp", [128, 3, 2048], BF16))
            w1l = enF(nc.sbuf_tensor("w1l", [128, 3, 2048], F32))
            h1o = enF(nc.sbuf_tensor("h1o", [128, 2, NTOK], FP16))
            xq1T = enF(nc.sbuf_tensor("xq1T", [128, 16, NTOK], BF16))
            with ExitStack() as pst:
                ph1 = [pst.enter_context(nc.psum_tensor(f"ph1_{i}", [128, 1024], F32)) for i in range(3)]
                ph = Phase(ctx)
                pv = {}
                xl = ph.add("sync", (lambda s: [DL(s, xq1T[:, :, t * 128:(t + 1) * 128], xq1d[t, :, :, :])
                                                for t in range(NT)] and None), dmas=NT)
                ph.add("sync", (lambda s: None), deps=[xl])
                def f1_prep(c):
                    Lw = ph.add("sync", (lambda s, c=c: DL(s, w1l[:, c % 3, 0:1024], w1p[:, c, 0:1024])),
                                deps=[pv.get(("trn", c - 3))], dmas=1)
                    Lw2 = ph.add("gpsimd", (lambda g, c=c: DG(g, w1l[:, c % 3, 1024:2048], w1p[:, c, 1024:2048])),
                                 deps=[pv.get(("trn", c - 3))], dmas=1)

                    def trn(v, c=c):
                        for hf in range(2):
                            sl = slice(hf * 1024, (hf + 1) * 1024)
                            v.tensor_scalar(scr0[:], w1l[:, c % 3, sl], tauc[:, 0:1], None, ALU.is_ge)
                            v.scalar_tensor_tensor(w1Tp[:, c % 3, sl], w1l[:, c % 3, sl], tauc[:, 1:2], scr0[:], ALU.is_ge, ALU.add)
                    TR = ph.add("vector", trn, deps=[Lw, Lw2, pv.get(("mm", c - 3))])
                    pv[("trn", c)] = TR

                    def mm(tn, c=c):
                        for kk in range(16):
                            for hf in range(2):
                                tn.matmul(ph1[c % 3][:, hf * 512:(hf + 1) * 512], w1Tp[:, c % 3, kk * 128:(kk + 1) * 128],
                                          xq1T[:, kk, hf * 512:(hf + 1) * 512],
                                          start=(kk == 0), stop=(kk == 15))
                    MM = ph.add("tensor", mm, deps=[TR, pv.get(("evu", c - 3))])
                    pv[("mm", c)] = MM

                def f1_consume(c):
                    def evu(v, c=c):
                        v.tensor_mul(scr1[:, 0:1024], ph1[c % 3][:], invrep[:])
                        v.tensor_sub(scr2[:, 0:1024], scr1[:, 0:1024], r2rep[:])
                    EV = ph.add("vector", evu, deps=[pv[("mm", c)], pv.get(("gel", c - 1))])
                    pv[("evu", c)] = EV
                    GE = ph.add("scalar", (lambda s, c=c: s.activation(h1o[:, c % 2, :], scr2[:, 0:1024], AF.Gelu)),
                                deps=[EV, pv.get(("h1w", c - 2)), pv.get(("va", c - 2))])
                    pv[("gel", c)] = GE

                    def va(v, c=c):
                        hc = h1o[:, c % 2, :]
                        if c == 0:
                            SV(v, v.tensor_scalar(absacc[:], hc, -1.0, None, ALU.mult))
                        else:
                            SV(v, v.scalar_tensor_tensor(absacc[:], hc, -1.0, absacc[:], ALU.mult, ALU.max))
                        v.tensor_tensor(absacc[:], absacc[:], hc, ALU.max)
                    pv[("va", c)] = ph.add("vector", va, deps=[GE])
                    pv[("h1w", c)] = ph.add("scalar", (lambda s, c=c: DA(s, h1gd[c, :, :], h1o[:, c % 2, :])),
                                            deps=[GE], dmas=1)
                for c in range(64):
                    f1_prep(c)
                    if c >= 1:
                        f1_consume(c - 1)
                f1_consume(63)
                ph.emit(nc)
            if upto >= 6:
                stF.close()

        if upto == 5:
            mark("d_xq1Ts", lambda: xq1T[:])
            mark("d_w1Tp", lambda: w1Tp[:])

        if upto >= 6:
            stF2 = ExitStack()
            enF2 = stF2.enter_context
            s2rep = enF2(nc.sbuf_tensor("s2rep", [128, NTOK], F32))
            with ExitStack() as pst:
                Pa = pst.enter_context(nc.psum_tensor("PaF", [128, 512], F32))
                prep2 = pst.enter_context(nc.psum_tensor("prepF", [128, 1024], F32))
                ph = Phase(ctx)
                T1 = ph.add("tensor", (lambda tn: [tn.transpose(Pa[:, i * 128:(i + 1) * 128],
                                                                absacc[:, i * 128:(i + 1) * 128], sidf[:])
                                                   for i in range(4)] and None))
                C1 = ph.add("scalar", (lambda s: s.copy(scr0[:, 0:512], Pa[:])), deps=[T1])
                T2 = ph.add("tensor", (lambda tn: [tn.transpose(Pa[:, i * 128:(i + 1) * 128],
                                                                absacc[:, (4 + i) * 128:(5 + i) * 128], sidf[:])
                                                   for i in range(4)] and None), deps=[C1])
                C2 = ph.add("scalar", (lambda s: s.copy(scr0[:, 512:1024], Pa[:])), deps=[T2])

                def red2(v):
                    for tt in range(NT):
                        ii = v.tensor_reduce(scr1[:, tt:tt + 1], scr0[:, tt * 128:(tt + 1) * 128], AX.X, ALU.max)
                    SV(v, ii)
                    v.tensor_scalar(scr1[:, 0:NT], scr1[:, 0:NT], 1e-5, None, ALU.max)
                def v3f(v):
                    SV(v, v.reciprocal(scr0[:, 0:NT], scr1[:, 0:NT]))
                    SV(v, v.tensor_scalar(s2c[:, 0:NT], scr0[:, 0:NT], 127.0, None, ALU.mult))
                    SV(v, v.tensor_scalar_mul(scr0[:, 0:NT], s2c[:, 0:NT], tauc[:, 5:6]))
                    v.reciprocal(inv2c[:, 0:NT], scr0[:, 0:NT])
                R2 = ph.add("vector", red2, deps=[C2])
                C4 = ph.add("vector", v3f, deps=[R2])
                C5 = C4
                for bb in range(2):
                    def trb2(tn, bb=bb):
                        for c in range(4):
                            tn.transpose(Pa[0:1, c * 128:(c + 1) * 128], s2c[:, bb * 4 + c:bb * 4 + c + 1], sidf[:])
                    T3 = ph.add("tensor", trb2, deps=[C5])
                    C5 = ph.add("scalar", (lambda s, bb=bb: s.copy(zrowf[0:1, bb * 512:(bb + 1) * 512], Pa[0:1, 0:512])),
                                deps=[T3])

                def rank3(tn):
                    for tt in range(NT):
                        tn.matmul(prep2[:, tt * 128:(tt + 1) * 128], sones1f[:],
                                  zrowf[0:1, tt * 128:(tt + 1) * 128], start=True, stop=True)
                T4 = ph.add("tensor", rank3, deps=[C5])
                ph.add("scalar", (lambda s: s.copy(s2rep[:], prep2[:])), deps=[T4])
                ph.emit(nc)

            xt = enF2(nc.sbuf_tensor("xt", [128, 4, NTOK], FP16))
            with ExitStack() as pst:
                pv2 = [pst.enter_context(nc.psum_tensor(f"pv2_{i}", [128, 512], F32)) for i in range(8)]
                ph = Phase(ctx)
                pv = {}
                def f2b_prep(c):
                    Lh = ph.add("gpsimd", (lambda g, c=c: DG(g, xt[:, c % 4, :], h1gd[c, :, :])),
                                deps=[pv.get(("xqw", c - 4))], dmas=1)

                    def qmul(v, c=c):
                        v.tensor_mul(scr1[:, (c % 2) * 1024:(c % 2) * 1024 + 1024], xt[:, c % 4, :], s2rep[:])
                    QM = ph.add("vector", qmul, deps=[Lh, pv.get(("qu", c - 2))])
                    QC_ = ph.add("scalar", (lambda s, c=c: s.activation(
                        scri[:, c % 2, :], scr1[:, (c % 2) * 1024:(c % 2) * 1024 + 1024],
                        AF.Identity, bias=scvals[:, 4:5])),
                        deps=[QM, pv.get(("qu", c - 2))])
                    pv[("qc", c)] = QC_

                def f2b_consume(c):
                    def quant(v, c=c):
                        sl1 = scr1[:, (c % 2) * 1024:(c % 2) * 1024 + 1024]
                        v.scalar_tensor_tensor(scr0[:, 0:1024], sl1, RBIAS, scri[:, c % 2, :], ALU.add, ALU.subtract)
                        v.tensor_scalar(scr2[:, 0:1024], scr0[:, 0:1024], 1.0, None, ALU.is_ge)
                        v.scalar_tensor_tensor(scr2[:, 0:1024], scr0[:, 0:1024], 0.0, scr2[:, 0:1024], ALU.is_ge, ALU.add)
                        v.scalar_tensor_tensor(xt[:, c % 4, :], scr2[:, 0:1024], -257.0, scri[:, c % 2, :], ALU.add, ALU.add)
                    QU = ph.add("vector", quant, deps=[pv[("qc", c)]])
                    pv[("qu", c)] = QU

                    def v2mm(tn, c=c):
                        for t in range(NT):
                            tn.matmul(pv2[t][:, 0:1], xt[:, c % 4, t * 128:(t + 1) * 128], sonescolh[:],
                                      start=(c == 0), stop=(c == 63))
                    VM = ph.add("tensor", v2mm, deps=[QU])
                    pv[("xqw", c)] = ph.add("scalar", (lambda s, c=c: DA(s, xq2d[c, :, :], xt[:, c % 4, :])),
                                            deps=[VM], dmas=1)
                for c in range(64):
                    f2b_prep(c)
                    if c >= 1:
                        f2b_consume(c - 1)
                f2b_consume(63)

                def vev(s):
                    for t in range(NT):
                        s.copy(v2s[:, t:t + 1], pv2[t][:, 0:1])
                EVV = ph.add("scalar", vev, deps=[pv[("xqw", 63)]])
                def vb2(v):
                    SV(v, v.tensor_mul(scr0[:, 0:NT], v2s[:, 0:NT], inv2c[:, 0:NT]))
                    v.tensor_scalar(bias2c[:, 0:NT], scr0[:, 0:NT], -1.0, None, ALU.mult)
                ph.add("vector", vb2, deps=[EVV])
                ph.emit(nc)

            w2s = enF2(nc.sbuf_tensor("w2s", [128, 2, 512], FP16))
            xs = enF2(nc.sbuf_tensor("xs", [128, 2, 512], F32))
            xqs = enF2(nc.sbuf_tensor("xqs", [128, 2, NTOK], FP16))
            ob = enF2(nc.sbuf_tensor("ob", [128, 2, 512], F32))
            with ExitStack() as pst:
                Pw = [pst.enter_context(nc.psum_tensor(f"PwF{i}", [128, 512], F32)) for i in range(NT)]
                for nq in range(4):
                    ph = Phase(ctx)
                    pv = {}
                    for kc in range(64):
                        Lw = ph.add("sync", (lambda s, nq=nq, kc=kc: DL(
                            s, ld0[:, kc % 2, 0:512], w2p[:, nq, kc, :])),
                            deps=[pv.get(("t2", nq, kc - 2)) if kc >= 2 else pv.get(("t2", nq - 1, 62 + kc))],
                            dmas=1)

                        def trn2(v, kc=kc):
                            v.tensor_scalar(scr0[:, 0:512], ld0[:, kc % 2, 0:512], tauc[:, 2:3], None, ALU.is_ge)
                            v.scalar_tensor_tensor(w2s[:, kc % 2, :], ld0[:, kc % 2, 0:512], tauc[:, 3:4],
                                                   scr0[:, 0:512], ALU.is_ge, ALU.add)
                        TN = ph.add("vector", trn2,
                                    deps=[Lw, pv.get(("mm", nq, kc - 2)) if kc >= 2 else pv.get(("mm", nq - 1, 62 + kc))])
                        pv[("t2", nq, kc)] = TN

                        Lx = ph.add("gpsimd", (lambda g, kc=kc: DG(g, xqs[:, kc % 2, :], xq2d[kc, :, :])),
                                    deps=[pv.get(("mm", nq, kc - 2)) if kc >= 2 else pv.get(("mm", nq - 1, 62 + kc))],
                                    dmas=1)

                        def mm2(tn, kc=kc):
                            for t in range(NT):
                                tn.matmul(Pw[t][:, :], xqs[:, kc % 2, t * 128:(t + 1) * 128],
                                          w2s[:, kc % 2, :], start=(kc == 0), stop=(kc == 63))
                        pv[("mm", nq, kc)] = ph.add("tensor", mm2,
                                                    deps=[TN, Lx, pv.get(("fin", nq * NT - 1))])
                    for t in range(NT):
                        u = nq * NT + t
                        Lxr = ph.add("gpsimd", (lambda g, t=t, nq=nq, u=u: DG(
                            g, xs[:, u % 2, :], hkv[512 + t * 128:512 + (t + 1) * 128, nq * 512:(nq + 1) * 512])),
                            deps=[pv.get(("fin", u - 2))], dmas=1)
                        CE = ph.add("scalar", (lambda s, t=t, nq=nq: s.activation(
                            scr2[:, 0:512], Pw[t][:, :], AF.Identity, bias=bias2c[:, t:t + 1],
                            scale=inv2c[:, t:t + 1])), deps=[pv[("mm", nq, 63)], pv.get(("fin", u - 1))])

                        def fin(v, t=t, nq=nq, u=u):
                            v.tensor_add(scr1[:, 0:512], scr2[:, 0:512], xs[:, u % 2, :])
                            v.tensor_add(ob[:, u % 2, :], scr1[:, 0:512], yac[:, t, nq * 512:(nq + 1) * 512])
                        FN = ph.add("vector", fin, deps=[CE, Lxr, pv.get(("ot", u - 2))])
                        pv[("fin", u)] = FN
                        OT = ph.add("sync", (lambda s, t=t, nq=nq, u=u: DO(
                            s, y[t * 128:(t + 1) * 128, nq * 512:(nq + 1) * 512], ob[:, u % 2, :])), deps=[FN])
                        pv[("ot", u)] = OT
                    ph.emit(nc)
            stF2.close()

        ph = Phase(ctx)

        def fino(s):
            for name, apf in dump_specs:
                DO(s, dbg_t[name].ap(), apf())
            if upto < 6:
                for t in range(NT):
                    DO(s, y[t * 128:(t + 1) * 128, 0:1024], scr1[:, 0:1024])
                    DO(s, y[t * 128:(t + 1) * 128, 1024:2048], scr1[:, 1024:2048])
            for q in ("sync", "gpsimd", "scalar"):
                n = dctr[q].v
                for r in range(NPOOL):
                    cnt = n // NPOOL + (1 if n % NPOOL > r else 0)
                    if cnt:
                        s.wait_ge(dsem[q][r], 16 * cnt)
        ph.add("sync", fino)
        ph.emit(nc)
        if upto < 3:
            stA.close()
        if upto < 5:
            st1.close()
        if upto == 5:
            stF.close()
    return nc, dbg_t



_W1P = [None]
_W2P = [None]


def prep_weight_perms(inputs):
    w1 = np.asarray(inputs["L0_ffn_w1"], np.float32)
    w2 = np.asarray(inputs["L0_ffn_w2"], np.float32)
    _W1P[0] = np.ascontiguousarray(
        w1.reshape(64, 128, 16, 128).transpose(3, 0, 2, 1).reshape(128, 64, 2048))
    _W2P[0] = np.ascontiguousarray(
        w2.reshape(4, 512, 64, 128).transpose(3, 0, 2, 1))


def prep_core_inputs(core, hf, inputs):
    if _W1P[0] is None:
        prep_weight_perms(inputs)
    half = core % 2
    seq = core // 2
    base = seq * T_SEQ + half * NTOK
    if half == 0:
        hkv = np.concatenate([np.zeros((512, HID), np.float32),
                              hf[base:base + NTOK]], axis=0)
    else:
        hkv = np.ascontiguousarray(hf[base - 512:base + NTOK])

    pos = np.maximum(np.arange(NKV, dtype=np.int64) + (half * NTOK - 512), 0).astype(np.float64)
    inv = 1.0 / (10000.0 ** (np.arange(0, 64, 2, dtype=np.float64) / 64.0))
    ang = pos[:, None] * inv[None, :]
    cosk = np.cos(ang).astype(np.float32)
    sinkt = np.sin(ang).astype(np.float32)
    cosq = np.ascontiguousarray(np.tile(cosk[512:, None, :], (1, 4, 1)).astype(np.float32))
    sinq = np.ascontiguousarray(np.tile(sinkt[512:, None, :], (1, 4, 1)).astype(np.float32))

    j = np.arange(NT)[:, None, None, None]
    r = np.arange(128)[None, :, None, None]
    c = np.arange(5)[None, None, :, None]
    kk = np.arange(128)[None, None, None, :]
    kpos = half * NTOK - 512 + (j + c) * 128 + kk
    qp = half * NTOK + j * 128 + r
    valid = ((qp - kpos) >= 0) & ((qp - kpos) < WIN) & (kpos >= 0)
    add = np.where(valid, np.float32(0), np.float32(-60000.0)).astype(BFD)
    mtm = np.ascontiguousarray(add.reshape(NT, 128, 640))
    mdt = np.ascontiguousarray(np.transpose(add, (0, 3, 2, 1)))

    return {
        "hkv": hkv,
        "wqc": inputs["L0_attn_wq_comp"], "wqu": inputs["L0_attn_wq_up"],
        "wk": inputs["L0_attn_wk"], "wv": inputs["L0_attn_wv"],
        "ow1": inputs["L0_attn_out_w1"], "ow2": inputs["L0_attn_out_w2"],
        "w1p": _W1P[0], "w2p": _W2P[0],
        "qwr": np.ascontiguousarray(np.tile(np.asarray(inputs["L0_attn_q_norm_w"], np.float32)[None, :], (128, 4))),
        "kwr": np.ascontiguousarray(np.tile(np.asarray(inputs["L0_attn_k_norm_w"], np.float32)[None, :], (128, 1))),
        "sinkr": np.ascontiguousarray(np.tile(np.asarray(inputs["L0_attn_sink_logit"], np.float32)[None, :], (128, 1))),
        "nsinkr": np.ascontiguousarray(np.tile(-np.asarray(inputs["L0_attn_sink_logit"], np.float32)[None, :], (128, 1))),
        "cosq": cosq, "sinq": sinq, "cosk": cosk, "sinkt": sinkt,
        "mtm": mtm, "mdt": mdt,
        "idb": np.eye(128, dtype=BFD),
        "idf": np.eye(128, dtype=np.float32),
        "idh": np.eye(128, dtype=np.float16),
        "ones1f": np.ones((1, 128), np.float32),
        "onescolh": np.ones((128, 1), np.float16),
        "cvals": np.tile(np.array([[1e-5, 1e-6, 128e-6, 0.0, 256.5]], np.float32), (128, 1)),
        "onescolf": np.ones((128, 1), np.float32),
    }


def kernel(**inputs):
    hf = np.ascontiguousarray(np.asarray(inputs["L0_attn_h"], np.float32).reshape(4 * T_SEQ, HID))
    prep_weight_perms(inputs)
    nc, _ = build()
    in_maps = [prep_core_inputs(cc, hf, inputs) for cc in range(8)]
    res = run_bass_kernel_spmd(nc, in_maps, core_ids=list(range(8)))
    out = np.concatenate([np.asarray(r["y"], np.float32) for r in res.results], axis=0)
    return out.reshape(4, T_SEQ, HID)


# version 13 (final)
# speedup vs baseline: 1.0266x; 1.0266x over previous
import sys

for _p in ("/opt/trn_rl_repo",):
    if _p not in sys.path:
        sys.path.insert(0, _p)

import numpy as np
import ml_dtypes
from contextlib import ExitStack

import concourse.bass as bass
import concourse.mybir as mybir
from concourse.bass_utils import run_bass_kernel_spmd

F32 = mybir.dt.float32
BF16 = mybir.dt.bfloat16
FP16 = mybir.dt.float16
I32 = mybir.dt.int32
AF = mybir.ActivationFunctionType
ALU = mybir.AluOpType
AX = mybir.AxisListType

HID = 2048
H, D, QC = 16, 128, 512
WIN = 512
G, INTER = 4, 2048
DFF = 8192
NTOK = 1024
NKV = 1536
NT = 8
NKT = 12
T_SEQ = 2048
QSC = float(1.0 / np.sqrt(np.float32(D)))
RBIAS = 256.5
BFD = ml_dtypes.bfloat16

ENGS = ("sync", "gpsimd", "tensor", "vector", "scalar")


class Ctr:
    def __init__(self):
        self.v = 0


class Phase:

    def __init__(self, ctx):
        self.ctx = ctx
        self.steps = []

    def add(self, eng, fn, deps=(), dmas=0):
        self.steps.append((eng, fn, [d for d in deps if d is not None], dmas))
        return len(self.steps) - 1

    def emit(self, nc):
        ctx = self.ctx
        sem, ctr, dsem, dctr = ctx["sem"], ctx["ctr"], ctx["dsem"], ctx["dctr"]
        NPOOL = 12
        stepval, dmaval = {}, {}
        c_eng = {e: ctr[e].v for e in ENGS}
        c_dma = {e: dctr[e].v for e in dctr}
        for i, (e, fn, deps, dmas) in enumerate(self.steps):
            c_eng[e] += 1
            stepval[i] = c_eng[e]
            if dmas:
                ws = {}
                for k in range(dmas):
                    idx = c_dma[e] + k
                    r = idx % NPOOL
                    ws[r] = 16 * (idx // NPOOL + 1)
                c_dma[e] += dmas
                dmaval[i] = (e, ws)
        present = [e for e in ENGS if any(s[0] == e for s in self.steps)]
        base = {e: ctr[e].v for e in ENGS}
        from concourse.bass_core_rust import BassInstruction as _BI
        lastins = [None]

        def _cb(ins, *a, **kw):
            lastins[0] = ins
        nc.push_inst_callback(_cb)
        try:
            with nc.Block() as blk:
                for ename in present:
                    def make(ename):
                        def body(eng):
                            for oe in ENGS:
                                if oe != ename and base[oe] > 0:
                                    eng.wait_ge(sem[oe], base[oe])
                            for i, (e, fn, deps, dmas) in enumerate(self.steps):
                                if e != ename:
                                    continue
                                waits = {}
                                for d in deps:
                                    de, _, _, ddma = self.steps[d]
                                    if ddma:
                                        qe, ws = dmaval[d]
                                        for r, v in ws.items():
                                            k = ("D", qe, r)
                                            waits[k] = max(waits.get(k, 0), v)
                                    elif de != ename:
                                        k = ("S", de, 0)
                                        waits[k] = max(waits.get(k, 0), stepval[d])
                                if ename in ("vector", "scalar") and stepval[i] - 1 > 0:
                                    eng.wait_ge(sem[ename], stepval[i] - 1)
                                for (kind, de, r), v in sorted(waits.items()):
                                    eng.wait_ge(dsem[de][r] if kind == "D" else sem[de], v)
                                if dmas:
                                    n0 = dctr[e].v
                                lastins[0] = None
                                fn(eng)
                                if dmas:
                                    assert dctr[e].v - n0 == dmas, f"step {i} dmas"
                                    eng.sem_inc(sem[ename], 1)
                                elif ename in ("tensor", "vector", "scalar") and lastins[0] is not None:
                                    _BI(lastins[0]).then_inc(sem[ename], 1)
                                else:
                                    eng.sem_inc(sem[ename], 1)
                                ctr[ename].v += 1
                        return body
                    getattr(blk, ename)(make(ename))
        finally:
            nc.remove_inst_callback(_cb)
        for e in ENGS:
            assert ctr[e].v == c_eng[e], (e, ctr[e].v, c_eng[e])


DBG_D = None
DBG_SB = 99
DBG_SA = 99


def build(upto=99, dbg=()):
    dqn, dtl, dow = DBG_D if DBG_D else (4, NT, True)
    nc = bass.Bass("TRN2", target_bir_lowering=False, detect_race_conditions=False)

    hkv = nc.dram_tensor("hkv", [NKV, HID], F32, kind="ExternalInput")
    wqc = nc.dram_tensor("wqc", [HID, QC], F32, kind="ExternalInput")
    wqu = nc.dram_tensor("wqu", [QC, H * D], F32, kind="ExternalInput")
    wk = nc.dram_tensor("wk", [HID, D], F32, kind="ExternalInput")
    wv = nc.dram_tensor("wv", [HID, D], F32, kind="ExternalInput")
    ow1 = nc.dram_tensor("ow1", [G, QC, INTER], F32, kind="ExternalInput")
    ow2 = nc.dram_tensor("ow2", [DFF, HID], F32, kind="ExternalInput")
    w1p = nc.dram_tensor("w1p", [128, 64, 2048], F32, kind="ExternalInput")
    w2p = nc.dram_tensor("w2p", [128, 4, 64, 512], F32, kind="ExternalInput")
    qwr = nc.dram_tensor("qwr", [128, 4 * D], F32, kind="ExternalInput")
    kwr = nc.dram_tensor("kwr", [128, D], F32, kind="ExternalInput")
    sinkr = nc.dram_tensor("sinkr", [128, H], F32, kind="ExternalInput")
    nsinkr = nc.dram_tensor("nsinkr", [128, H], F32, kind="ExternalInput")
    cosq = nc.dram_tensor("cosq", [NTOK, 4, 32], F32, kind="ExternalInput")
    sinq = nc.dram_tensor("sinq", [NTOK, 4, 32], F32, kind="ExternalInput")
    cosk = nc.dram_tensor("cosk", [NKV, 32], F32, kind="ExternalInput")
    sinkt = nc.dram_tensor("sinkt", [NKV, 32], F32, kind="ExternalInput")
    mtm = nc.dram_tensor("mtm", [NT, 128, 640], BF16, kind="ExternalInput")
    mdt = nc.dram_tensor("mdt", [NT, 128, 5, 128], BF16, kind="ExternalInput")
    idb = nc.dram_tensor("idb", [128, 128], BF16, kind="ExternalInput")
    idf = nc.dram_tensor("idf", [128, 128], F32, kind="ExternalInput")
    idh = nc.dram_tensor("idh", [128, 128], FP16, kind="ExternalInput")
    ones1f = nc.dram_tensor("ones1f", [1, 128], F32, kind="ExternalInput")
    onescolh = nc.dram_tensor("onescolh", [128, 1], FP16, kind="ExternalInput")
    onescolf = nc.dram_tensor("onescolf", [128, 1], F32, kind="ExternalInput")
    cvals = nc.dram_tensor("cvals", [128, 5], F32, kind="ExternalInput")

    y = nc.dram_tensor("y", [NTOK, HID], F32, kind="ExternalOutput")
    xq1d = nc.dram_tensor("xq1d", [NT, 128, 16, 128], BF16, kind="Internal")
    a8d = nc.dram_tensor("a8d", [16, 128, NTOK], BF16, kind="Internal")
    h1gd = nc.dram_tensor("h1gd", [64, 128, NTOK], FP16, kind="Internal")
    xq2d = nc.dram_tensor("xq2d", [64, 128, NTOK], FP16, kind="Internal")

    dbg_t = {name: nc.dram_tensor(name, list(shape), dt, kind="ExternalOutput")
             for name, shape, dt in dbg}

    stP = ExitStack()
    en = stP.enter_context
    with stP:
        sqwr = en(nc.sbuf_tensor("sqwr", [128, 4 * D], F32))
        skwr = en(nc.sbuf_tensor("skwr", [128, D], F32))
        ssinkc = en(nc.sbuf_tensor("ssinkc", [128, H], F32))
        snsink = en(nc.sbuf_tensor("snsink", [128, H], F32))
        ssinkexp = en(nc.sbuf_tensor("ssinkexp", [128, H], F32))
        sidb = en(nc.sbuf_tensor("sidb", [128, 128], BF16))
        sidf = en(nc.sbuf_tensor("sidf", [128, 128], F32))
        sidh = en(nc.sbuf_tensor("sidh", [128, 128], FP16))
        sones1f = en(nc.sbuf_tensor("sones1f", [1, 128], F32))
        sonescolh = en(nc.sbuf_tensor("sonescolh", [128, 1], FP16))
        sonescolf = en(nc.sbuf_tensor("sonescolf", [128, 1], F32))
        scvals = en(nc.sbuf_tensor("scvals", [128, 5], F32))
        ld0 = en(nc.sbuf_tensor("ld0", [128, 2, HID], F32))
        scr0 = en(nc.sbuf_tensor("scr0", [128, 1024], F32))
        scr1 = en(nc.sbuf_tensor("scr1", [128, HID], F32))
        scr2 = en(nc.sbuf_tensor("scr2", [128, 1024], F32))
        scri = en(nc.sbuf_tensor("scri", [128, 2, 1024], I32))
        scrbq = en(nc.sbuf_tensor("scrbq", [128, 2, HID], BF16))
        jnkA = en(nc.sbuf_tensor("jnkA", [128, 1024], F32))
        jnkB = en(nc.sbuf_tensor("jnkB", [128, HID], F32))
        kout = en(nc.sbuf_tensor("kout", [128, 128], BF16))
        tiny = en(nc.sbuf_tensor("tiny", [128, 64], F32))
        s1c = en(nc.sbuf_tensor("s1c", [128, NT], F32))
        rsx = en(nc.sbuf_tensor("rsx", [128, NT], F32))
        ssqw = en(nc.sbuf_tensor("ssqw", [128, 256], F32))
        ssqw2 = en(nc.sbuf_tensor("ssqw2", [128, 256], F32))
        tauc = en(nc.sbuf_tensor("tauc", [128, 8], F32))
        zrow = en(nc.sbuf_tensor("zrow", [16, 128], F32))
        zrowf = en(nc.sbuf_tensor("zrowf", [1, 2048], F32))
        s2c = en(nc.sbuf_tensor("s2c", [128, NT], F32))
        inv2c = en(nc.sbuf_tensor("inv2c", [128, NT], F32))
        bias2c = en(nc.sbuf_tensor("bias2c", [128, NT], F32))
        v2s = en(nc.sbuf_tensor("v2s", [128, NT], F32))
        absacc = en(nc.sbuf_tensor("absacc", [128, NTOK], F32, side="right"))
        invrep = en(nc.sbuf_tensor("invrep", [128, NTOK], F32, side="right"))
        r2rep = en(nc.sbuf_tensor("r2rep", [128, NTOK], F32, side="right"))

        sem = {e: en(nc.semaphore("s_" + e)) for e in ENGS}
        NPOOL = 12
        dsem = {q: [en(nc.semaphore(f"d_{q}{r}")) for r in range(NPOOL)]
                for q in ("sync", "gpsimd", "scalar")}
        gsem = {q: en(nc.semaphore(f"g_{q}")) for q in ("sync", "gpsimd", "scalar")}
        ssem = {e: en(nc.semaphore(f"ss_{e}")) for e in ("vector", "scalar")}
        ssc = {e: Ctr() for e in ("vector", "scalar")}

        def SV(v, inst):
            ssc["vector"].v += 1
            inst.then_inc(ssem["vector"], 1)
            v.wait_ge(ssem["vector"], ssc["vector"].v)

        def SC(s, inst):
            ssc["scalar"].v += 1
            inst.then_inc(ssem["scalar"], 1)
            s.wait_ge(ssem["scalar"], ssc["scalar"].v)
        ctr = {e: Ctr() for e in ENGS}
        dctr = {"sync": Ctr(), "gpsimd": Ctr(), "scalar": Ctr()}
        gctr = {"sync": Ctr(), "gpsimd": Ctr(), "scalar": Ctr()}
        cO = Ctr()
        ctx = dict(sem=sem, ctr=ctr, dsem=dsem, dctr=dctr)

        def _dma(eng, q, out, in_):
            idx = dctr[q].v
            r = idx % NPOOL
            if idx >= NPOOL:
                eng.wait_ge(dsem[q][r], 16 * (idx // NPOOL))
            eng.sem_inc(gsem[q], 1)
            gctr[q].v += 1
            inst = eng.dma_start(out, in_)
            inst._wait_ge(gsem[q], gctr[q].v)
            inst.then_inc(dsem[q][r], 16)
            dctr[q].v += 1

        def DL(s, out, in_):
            _dma(s, "sync", out, in_)

        def DG(g, out, in_):
            _dma(g, "gpsimd", out, in_)

        def DA(s, out, in_):
            _dma(s, "scalar", out, in_)

        def DO(s, out, in_):
            _dma(s, "sync", out, in_)
            cO.v += 1

        TC = {k: i for i, k in enumerate(
            ["hsum", "ssq", "m", "m2", "var", "rstd", "mb", "negmb", "am",
             "amx", "rs1", "t1", "bb", "kssq", "krstd", "S1", "S2",
             "rms1", "rms2", "qs0", "qs1", "qs2", "qs3", "nm", "den",
             "sinkp", "den2", "lnd", "x0", "x1", "x2", "x3", "x4", "x5",
             "dn0", "dn1", "dn2", "dn3", "z0", "z1", "z2", "z3"])}

        def tc(n):
            return tiny[:, TC[n]:TC[n] + 1]

        dump_specs = []

        def mark(name, apf):
            if name in dbg_t:
                dump_specs.append((name, apf))

        ph = Phase(ctx)

        def ld_consts(s):
            for dst, src in [
                (sqwr[:], qwr[:, :]), (skwr[:], kwr[:, :]),
                (ssinkc[:], sinkr[:, :]), (snsink[:], nsinkr[:, :]),
                (sidb[:], idb[:, :]), (sidf[:], idf[:, :]), (sidh[:], idh[:, :]),
                (sones1f[:], ones1f[:, :]), (sonescolh[:], onescolh[:, :]),
                (sonescolf[:], onescolf[:, :]), (scvals[:], cvals[:, :]),
            ]:
                DL(s, dst, src)
        c0 = ph.add("sync", ld_consts, dmas=11)
        ph.add("sync", (lambda s: None), deps=[c0])
        ph.emit(nc)

        st1 = ExitStack()
        qmidT = st1.enter_context(nc.sbuf_tensor("qmidT", [128, 4, NTOK], BF16))
        kTs = st1.enter_context(nc.sbuf_tensor("kTs", [128, NKT, 128], BF16))
        vsb = st1.enter_context(nc.sbuf_tensor("vsb", [128, NKT, 128], BF16))

        stA = ExitStack()
        enA = stA.enter_context
        swqc = enA(nc.sbuf_tensor("swqc", [128, 16, QC], BF16))
        swkv = enA(nc.sbuf_tensor("swkv", [128, 16, 256], BF16))
        xnT = enA(nc.sbuf_tensor("xnT", [128, 16, NKV], BF16))
        scrb = enA(nc.sbuf_tensor("scrb", [128, 2, HID], BF16))
        xqst = enA(nc.sbuf_tensor("xqst", [128, 2, 16, 128], BF16))
        scosk = enA(nc.sbuf_tensor("scosk", [128, NKT, 32], F32))
        ssinkt = enA(nc.sbuf_tensor("ssinkt", [128, NKT, 32], F32))

        ph = Phase(ctx)

        def ld_ropek(s):
            for j in range(NKT):
                DL(s, scosk[:, j, :], cosk[j * 128:(j + 1) * 128, :])
                DL(s, ssinkt[:, j, :], sinkt[j * 128:(j + 1) * 128, :])
        ck = ph.add("sync", ld_ropek, dmas=2 * NKT)
        ph.add("sync", (lambda s: None), deps=[ck])
        ph.add("scalar", (lambda s: s.activation(ssinkexp[:], ssinkc[:], AF.Exp)))
        wplan = []
        for i in range(16):
            wplan.append((swqc[:, i, :], wqc[i * 128:(i + 1) * 128, :], QC, None, None))
        for i in range(16):
            wplan.append((swkv[:, i, 0:128], wk[i * 128:(i + 1) * 128, :], D,
                          swkv[:, i, 128:256], wv[i * 128:(i + 1) * 128, :]))
        prevc = {}
        for i, (dst, src, ncol, dst2, src2) in enumerate(wplan):
            nd = 1 if dst2 is None else 2

            def ldw(s, i=i, src=src, ncol=ncol, src2=src2, nd=nd):
                DL(s, ld0[:, i % 2, 0:ncol], src)
                if nd == 2:
                    DL(s, ld0[:, i % 2, ncol:2 * ncol], src2)
            lid = ph.add("sync", ldw, deps=[prevc.get(i - 2)], dmas=nd)

            def cst(v, i=i, dst=dst, ncol=ncol, dst2=dst2):
                v.tensor_copy(dst, ld0[:, i % 2, 0:ncol])
                if dst2 is not None:
                    v.tensor_copy(dst2, ld0[:, i % 2, ncol:2 * ncol])
            prevc[i] = ph.add("vector", cst, deps=[lid])
        ph.emit(nc)

        stats_src = []
        for n in range(256):
            stats_src.append((w1p[:, n // 4, (n % 4) * 512:(n % 4 + 1) * 512], ssqw, n))
        for n in range(256):
            stats_src.append((w2p[:, n // 64, n % 64, :], ssqw2, n))
        nst = [0]
        sdep = {}

        def stats_slot(n):
            return ld0[:, n % 2, 1024 + ((n // 2) % 2) * 512:1536 + ((n // 2) % 2) * 512]

        spend = []

        def _stats_sq(ph_):
            m, lstep = spend.pop(0)
            srcap, out, col = stats_src[m]
            if m % 2 == 0:
                st = ph_.add("scalar", (lambda s, m=m, out=out, col=col: s.activation(
                    stats_slot(m), stats_slot(m), AF.Square,
                    accum_out=out[:, col:col + 1])), deps=[lstep])
            else:
                st = ph_.add("vector", (lambda v, m=m, out=out, col=col: v.scalar_tensor_tensor(
                    scr0[:, 0:512], stats_slot(m), 1.0, stats_slot(m), ALU.mult, ALU.mult,
                    accum_out=out[:, col:col + 1])), deps=[lstep])
            sdep[("sq", m, "ph")] = ph_
            sdep[("sq", m)] = st

        def emit_stats(ph_, k):
            if upto < 3:
                return
            for _ in range(k):
                n = nst[0]
                if n >= 512:
                    return
                if len(spend) >= 3:
                    _stats_sq(ph_)
                srcap, out, col = stats_src[n]
                wdep = sdep.get(("sq", n - 4)) if sdep.get(("sq", n - 4, "ph")) is ph_ else None
                lstep = ph_.add("scalar", (lambda s, n=n, srcap=srcap: DA(
                    s, stats_slot(n), srcap)),
                    deps=[wdep], dmas=1)
                spend.append((n, lstep))
                nst[0] += 1

        def emit_stats_flush(ph_):
            while spend:
                _stats_sq(ph_)

        NKT_B = 0 if upto < 1 else NKT
        with ExitStack() as pst:
            ptr = pst.enter_context(nc.psum_tensor("ptrB", [128, 1024], BF16))
            ptr2 = pst.enter_context(nc.psum_tensor("ptrB2", [128, 1024], BF16))
            pq = pst.enter_context(nc.psum_tensor("pqC", [128, 1024], F32))
            pkv = [pst.enter_context(nc.psum_tensor(f"pkvC{i}", [128, 256], F32)) for i in range(2)]
            ptrc = pst.enter_context(nc.psum_tensor("ptrC", [128, 128], BF16))
            ph = Phase(ctx)
            pv = {}

            for j in range(NKT_B):
                jo = j - 4
                Lj = ph.add("sync", (lambda s, j=j: DL(s, ld0[:, j % 2, :], hkv[j * 128:(j + 1) * 128, :])),
                            deps=[pv.get(("fl", j - 2))], dmas=1)
                hb = ld0[:, j % 2, :]

                def c1(s, hb=hb, j=j):
                    s.activation(jnkB[:], hb, AF.Square, accum_out=tc("ssq"))
                    s.activation(jnkB[:], hb, AF.Copy, accum_out=tc("hsum"))
                C1 = ph.add("scalar", c1, deps=[Lj])

                def v1(v):
                    SV(v, v.tensor_scalar(tc("m"), tc("hsum"), float(1.0 / HID), None, ALU.mult))
                    SV(v, v.tensor_mul(tc("m2"), tc("m"), tc("m")))
                    v.scalar_tensor_tensor(tc("var"), tc("ssq"), float(1.0 / HID), tc("m2"), ALU.mult, ALU.subtract)
                V1 = ph.add("vector", v1, deps=[C1])
                C2 = ph.add("scalar", (lambda s: s.activation(tc("x0"), tc("var"), AF.Sqrt, bias=scvals[:, 0:1])), deps=[V1])

                def v2(v):
                    SV(v, v.reciprocal(tc("rstd"), tc("x0")))
                    SV(v, v.tensor_mul(tc("mb"), tc("m"), tc("rstd")))
                    v.tensor_scalar(tc("negmb"), tc("mb"), -1.0, None, ALU.mult)
                V2 = ph.add("vector", v2, deps=[C2])
                C3 = ph.add("scalar", (lambda s, hb=hb, j=j: s.activation(
                    scrb[:, j % 2, :], hb, AF.Identity, bias=tc("negmb"), scale=tc("rstd"))),
                    deps=[V2, pv.get(("P1", j - 2))])

                def p1(t, j=j):
                    for kk in range(16):
                        pt = ptr if kk < 8 else ptr2
                        t.transpose(pt[:, (kk % 8) * 128:(kk % 8 + 1) * 128],
                                    scrb[:, j % 2, kk * 128:(kk + 1) * 128], sidb[:])
                P1 = ph.add("tensor", p1, deps=[C3, pv.get(("C4", j - 1)), pv.get(("C7", j - 1))])

                def c4(s, j=j):
                    s.copy(xnT[:, 0:8, j * 128:(j + 1) * 128], ptr.ap().rearrange("p (a b) -> p a b", a=8))
                    s.copy(xnT[:, 8:16, j * 128:(j + 1) * 128], ptr2.ap().rearrange("p (a b) -> p a b", a=8))
                C4 = ph.add("scalar", c4, deps=[P1])
                pv[("C4", j)] = C4
                pv[("P1", j)] = P1

                if jo >= 0:
                    def v3(v, j=j, jo=jo):
                        SV(v, v.tensor_reduce(tc("am"), scrb[:, j % 2, :], AX.X, ALU.max, apply_absolute_value=True))
                        SV(v, v.tensor_scalar(tc("amx"), tc("am"), 1e-5, None, ALU.max))
                        SV(v, v.reciprocal(tc("x1"), tc("amx")))
                        v.tensor_scalar(s1c[:, jo:jo + 1], tc("x1"), 127.0, None, ALU.mult)
                    C5 = V3 = ph.add("vector", v3, deps=[C3])

                    def v4(v, jo=jo):
                        v.tensor_mul(tc("rs1"), tc("rstd"), s1c[:, jo:jo + 1])
                        SV(v, v.tensor_mul(tc("t1"), tc("mb"), s1c[:, jo:jo + 1]))
                        v.tensor_scalar(tc("bb"), tc("t1"), -1.0, RBIAS, ALU.mult, ALU.add)
                    V4 = ph.add("vector", v4, deps=[C5])
                    C6 = ph.add("scalar", (lambda s, hb=hb: SC(s, s.activation(
                        scr1[:], hb, AF.Identity, bias=tc("bb"), scale=tc("rs1")))),
                        deps=[V4, pv.get(("kvf", j - 1))])

                    def v5(v, jo=jo, j=j):
                        for hf in range(2):
                            s1h = scr1[:, hf * 1024:(hf + 1) * 1024]
                            v.tensor_copy(scri[:, 0, :], s1h)
                            v.scalar_tensor_tensor(scr2[:], s1h, 1.0, scri[:, 0, :], ALU.mult, ALU.subtract)
                            v.tensor_scalar(scr0[:], scr2[:], 1.0, None, ALU.is_ge)
                            v.scalar_tensor_tensor(scr0[:], scr2[:], 0.0, scr0[:], ALU.is_ge, ALU.add)
                            v.scalar_tensor_tensor(scr2[:], scr0[:], -257.0, scri[:, 0, :], ALU.add, ALU.add)
                            v.tensor_scalar(scrbq[:, j % 2, hf * 1024:(hf + 1) * 1024], scr2[:], -128.0, 127.0, ALU.max, ALU.min)
                        v.tensor_reduce(rsx[:, jo:jo + 1], scrbq[:, j % 2, :], AX.X, ALU.add)
                    V5 = ph.add("vector", v5, deps=[C6, pv.get(("P2", j - 1))])

                    def p2(t, j=j):
                        for kk in range(16):
                            pt = ptr if kk < 8 else ptr2
                            t.transpose(pt[:, (kk % 8) * 128:(kk % 8 + 1) * 128],
                                        scrbq[:, j % 2, kk * 128:(kk + 1) * 128], sidb[:])
                    P2 = ph.add("tensor", p2, deps=[V5, C4])

                    def c7(s, jo=jo, j=j):
                        s.copy(xqst[:, j % 2, 0:8, :], ptr.ap().rearrange("p (a b) -> p a b", a=8))
                        SC(s, s.copy(xqst[:, j % 2, 8:16, :], ptr2.ap().rearrange("p (a b) -> p a b", a=8)))
                        DA(s, xq1d[jo, :, :, :], xqst[:, j % 2, :, :])
                    C7 = ph.add("scalar", c7, deps=[P2, pv.get(("C7", j - 2))], dmas=1)
                    pv[("C7", j)] = C7
                    pv[("P2", j)] = P2
                    pv[("fl", j)] = C6
                else:
                    pv[("fl", j)] = C3

                if upto >= 2:
                    def kvmm(t, j=j):
                        for kk in range(16):
                            t.matmul(pkv[j % 2][:, :], xnT[:, kk, j * 128:(j + 1) * 128],
                                     swkv[:, kk, :], start=(kk == 0), stop=(kk == 15))
                    Pm = ph.add("tensor", kvmm, deps=[pv.get(("kvf", j - 2)), C4])

                    def kstat(s, j=j):
                        SC(s, s.activation(jnkA[:, 0:128], pkv[j % 2][:, 0:128], AF.Square, accum_out=tc("kssq")))
                        s.activation(tc("x2"), tc("kssq"), AF.Sqrt, scale=float(1.0 / D), bias=scvals[:, 1:2])
                        s.copy(vsb[:, j, :], pkv[j % 2][:, 128:256])
                    Cs = ph.add("scalar", kstat, deps=[Pm])

                    def krope(v, j=j):
                        SV(v, v.reciprocal(tc("krstd"), tc("x2")))
                        p = pkv[j % 2]
                        x1, x2 = p[:, 0:32], p[:, 32:64]
                        cs, sn = scosk[:, j, :], ssinkt[:, j, :]
                        v.tensor_mul(scr0[:, 0:32], x1, cs)
                        SV(v, v.tensor_mul(scr0[:, 32:64], x2, sn))
                        v.tensor_sub(scr1[:, 0:32], scr0[:, 0:32], scr0[:, 32:64])
                        v.tensor_mul(scr0[:, 0:32], x1, sn)
                        SV(v, v.tensor_mul(scr0[:, 32:64], x2, cs))
                        SV(v, v.tensor_add(scr1[:, 32:64], scr0[:, 0:32], scr0[:, 32:64]))
                        SV(v, v.tensor_scalar_mul(scr1[:, 0:64], scr1[:, 0:64], tc("krstd")))
                        SV(v, v.tensor_scalar_mul(scr1[:, 64:128], p[:, 64:128], tc("krstd")))
                        v.tensor_mul(kout[:], scr1[:, 0:128], skwr[:])
                    Vr = ph.add("vector", krope, deps=[Cs, pv.get(("ktr", j - 1)), V5 if jo >= 0 else None])
                    Pt = ph.add("tensor", (lambda t: t.transpose(ptrc[:], kout[:], sidb[:])),
                                deps=[Vr, pv.get(("ke", j - 1))])
                    pv[("ktr", j)] = Pt
                    Ce = ph.add("scalar", (lambda s, j=j: s.copy(kTs[:, j, :], ptrc[:])), deps=[Pt])
                    pv[("ke", j)] = Ce
                    pv[("kvf", j)] = Vr

            if upto >= 2:
                for qc in range(4):
                    def pmm(t, qc=qc):
                        for kk in range(16):
                            for hf in range(2):
                                t.matmul(pq[:, hf * 512:(hf + 1) * 512],
                                         swqc[:, kk, qc * 128:(qc + 1) * 128],
                                         xnT[:, kk, 512 + hf * 512:512 + (hf + 1) * 512],
                                         start=(kk == 0), stop=(kk == 15))
                    Pm = ph.add("tensor", pmm, deps=[pv.get(("qe", qc - 1)), pv.get(("C4", NKT_B - 1))])
                    pv[("qe", qc)] = ph.add("scalar", (lambda s, qc=qc: s.copy(qmidT[:, qc, :], pq[:])), deps=[Pm])
            ph.emit(nc)
            if upto == 3:
                ph = Phase(ctx)
                emit_stats(ph, 512)
                emit_stats_flush(ph)
                ph.emit(nc)

        mark("d_xnT", lambda: xnT[:])
        mark("d_s1c", lambda: s1c[:])
        mark("d_rsx", lambda: rsx[:])
        mark("d_xq1d7", lambda: xq1d[7, :, :, :])
        mark("d_qmidT", lambda: qmidT[:])
        mark("d_kT", lambda: kTs[:])
        mark("d_v", lambda: vsb[:])

        def emit_s2():
            with ExitStack() as pst:
                p1x1 = pst.enter_context(nc.psum_tensor("p1x1", [128, 512], F32))
                prep = pst.enter_context(nc.psum_tensor("prepS", [128, 1024], F32))
                ph = Phase(ctx)

                def red(v):
                    v.tensor_reduce(tc("x0"), ssqw[:], AX.X, ALU.add)
                    v.tensor_reduce(tc("x1"), ssqw2[:], AX.X, ALU.add)
                R1 = ph.add("vector", red)

                def m1(t):
                    t.matmul(p1x1[0:1, 0:1], sonescolf[:], tc("x0"), start=True, stop=True)
                    t.matmul(p1x1[0:1, 1:2], sonescolf[:], tc("x1"), start=True, stop=True)
                M1 = ph.add("tensor", m1, deps=[R1])
                Cm = ph.add("scalar", (lambda s: s.activation(
                    tiny[0:1, TC["S1"]:TC["S1"] + 2], p1x1[0:1, 0:2], AF.Copy, scale=float(1.0 / (DFF * HID)))), deps=[M1])
                Vm = ph.add("vector", (lambda v: v.tensor_scalar(
                    tiny[0:1, TC["S1"]:TC["S1"] + 2], tiny[0:1, TC["S1"]:TC["S1"] + 2], 1e-5, None, ALU.max)), deps=[Cm])

                def acts2(s):
                    s.activation(tiny[0:1, TC["x2"]:TC["x2"] + 1], tiny[0:1, TC["S1"]:TC["S1"] + 1], AF.Sqrt, scale=0.25)
                    SC(s, s.activation(tiny[0:1, TC["x3"]:TC["x3"] + 1], tiny[0:1, TC["S2"]:TC["S2"] + 1], AF.Sqrt, scale=0.25))
                    s.activation(tiny[0:1, TC["x4"]:TC["x4"] + 1], tiny[0:1, TC["x2"]:TC["x2"] + 1], AF.Copy, scale=-1.0)
                    s.activation(tiny[0:1, TC["x5"]:TC["x5"] + 1], tiny[0:1, TC["x3"]:TC["x3"] + 1], AF.Copy, scale=-1.0)
                Ca0 = ph.add("scalar", acts2, deps=[Vm])

                def vrms(v):
                    v.reciprocal(tiny[0:1, TC["x0"]:TC["x0"] + 1], tiny[0:1, TC["x2"]:TC["x2"] + 1])
                    SV(v, v.reciprocal(tiny[0:1, TC["x1"]:TC["x1"] + 1], tiny[0:1, TC["x3"]:TC["x3"] + 1]))
                    v.tensor_scalar(tiny[0:1, TC["rms1"]:TC["rms1"] + 1], tiny[0:1, TC["x0"]:TC["x0"] + 1], 0.5, None, ALU.mult)
                    v.tensor_scalar(tiny[0:1, TC["rms2"]:TC["rms2"] + 1], tiny[0:1, TC["x1"]:TC["x1"] + 1], 0.5, None, ALU.mult)
                Ca = ph.add("vector", vrms, deps=[Ca0])

                def bcast(t):
                    for col, src in [(0, "x2"), (1, "x4"), (2, "x3"), (3, "x5"), (4, "rms1"), (5, "rms2")]:
                        t.matmul(p1x1[:, 8 + col:9 + col], sones1f[:], tiny[0:1, TC[src]:TC[src] + 1],
                                 start=True, stop=True)
                Mb = ph.add("tensor", bcast, deps=[Ca])
                Cb = ph.add("scalar", (lambda s: s.copy(tauc[:, 0:6], p1x1[:, 8:14])), deps=[Mb])
                def vinv(v):
                    SV(v, v.tensor_scalar_mul(scr0[:, 0:NT], s1c[:, 0:NT], tauc[:, 4:5]))
                    SV(v, v.reciprocal(scr1[:, 0:NT], scr0[:, 0:NT]))
                    v.tensor_mul(scr1[:, NT:2 * NT], rsx[:, 0:NT], scr1[:, 0:NT])
                Vj = ph.add("vector", vinv, deps=[Cb])
                Pk = Vj
                for bb in range(4):
                    def trb(t, bb=bb):
                        for c in range(4):
                            t.transpose(p1x1[0:1, c * 128:(c + 1) * 128], scr1[:, bb * 4 + c:bb * 4 + c + 1], sidf[:])
                    Pk2 = ph.add("tensor", trb, deps=[Pk])
                    Pk = ph.add("scalar", (lambda s, bb=bb: s.copy(zrowf[0:1, bb * 512:(bb + 1) * 512], p1x1[0:1, 0:512])),
                                deps=[Pk2])

                def rank1(t):
                    for tt in range(NT):
                        t.matmul(prep[:, tt * 128:(tt + 1) * 128], sones1f[:], zrowf[0:1, tt * 128:(tt + 1) * 128],
                                 start=True, stop=True)
                Pr = ph.add("tensor", rank1, deps=[Pk])
                Cr = ph.add("scalar", (lambda s: s.copy(invrep[:], prep[:])), deps=[Pr])

                def rank2(t):
                    for tt in range(NT):
                        t.matmul(prep[:, tt * 128:(tt + 1) * 128], sones1f[:],
                                 zrowf[0:1, 1024 + tt * 128:1024 + (tt + 1) * 128], start=True, stop=True)
                Pr2 = ph.add("tensor", rank2, deps=[Cr])
                ph.add("scalar", (lambda s: s.copy(r2rep[:], prep[:])), deps=[Pr2])
                ph.emit(nc)

        yac = None
        if upto >= 3:
            stA.close()
            yac = en(nc.sbuf_tensor("yac", [128, NT, HID], FP16, side="right"))
            if upto == 3:
                emit_s2()

            mark("d_tauc", lambda: tauc[:])
            mark("d_s2c", lambda: s2c[:])
            mark("d_inv2c", lambda: inv2c[:])
            mark("d_bias2c", lambda: bias2c[:])
            mark("d_v2s", lambda: v2s[:])
            mark("d_absacc", lambda: absacc[:])
            mark("d_h1g", lambda: h1gd[0, :, :])
            mark("d_h1gL", lambda: h1gd[63, :, :])
            mark("d_xq2", lambda: xq2d[0, :, :])
            mark("d_invrep", lambda: invrep[:])
            mark("d_r2rep", lambda: r2rep[:])

        if upto >= 4:
            stB = ExitStack()
            enB = stB.enter_context
            scosq_s = enB(nc.sbuf_tensor("scosq", [128, 2, 4, 32], F32))
            ssinq_s = enB(nc.sbuf_tensor("ssinq", [128, 2, 4, 32], F32))
            swquq = enB(nc.sbuf_tensor("swquq", [128, 4, 512], BF16))
            smtm = enB(nc.sbuf_tensor("smtm", [128, 2, 640], BF16))
            w1g = enB(nc.sbuf_tensor("w1g", [128, 4, INTER], BF16))
            aT = enB(nc.sbuf_tensor("aT", [128, 4, NTOK], BF16))
            a8st = enB(nc.sbuf_tensor("a8st", [128, 2, NTOK], BF16))
            a8s = enB(nc.sbuf_tensor("a8s", [128, 4, NTOK], BF16))
            pTb = enB(nc.sbuf_tensor("pTb", [128, 5, 512], BF16))
            qnTt = enB(nc.sbuf_tensor("qnTt", [128, 4, 128], BF16))
            scrF = enB(nc.sbuf_tensor("scrF", [128, 2, 4, 640], FP16))
            scrP = enB(nc.sbuf_tensor("scrP", [128, 2, 640], F32))
            nzh = enB(nc.sbuf_tensor("nzh", [128, 16], F32))
            ow2b = enB(nc.sbuf_tensor("ow2b", [128, 4, 512], BF16))


            qro = enB(nc.sbuf_tensor("qro", [128, 512], FP16))
            for q in range(dqn):
                with ExitStack() as pst:
                    Pq2 = [pst.enter_context(nc.psum_tensor(f"PqD{q}_{i}", [128, 512], F32)) for i in range(2)]
                    Ptm = [pst.enter_context(nc.psum_tensor(f"PtmD{q}_{i}", [128, 640], F32)) for i in range(2)]
                    Ptc = pst.enter_context(nc.psum_tensor(f"PtcD{q}", [128, 512], FP16))
                    Pa = pst.enter_context(nc.psum_tensor(f"PaD{q}", [128, 512], F32))
                    ph = Phase(ctx)
                    pv = {}
                    for i in range(4):
                        def ldow(s, i=i, q=q):
                            DL(s, ld0[:, i % 2, :], ow1[q, i * 128:(i + 1) * 128, :])
                            DL(s, jnkA[:, 0:512], wqu[i * 128:(i + 1) * 128, q * 512:(q + 1) * 512])
                        Lw = ph.add("sync", ldow, deps=[pv.get(("w1c", i - 1))], dmas=2)

                        def cstow(v, i=i):
                            v.tensor_copy(w1g[:, i, :], ld0[:, i % 2, :])
                            v.tensor_copy(swquq[:, i, :], jnkA[:, 0:512])
                        pv[("w1c", i)] = ph.add("vector", cstow, deps=[Lw])

                    def stage_b(t):
                        for c in range(5):
                            def ptr_(tn, c=c, t=t):
                                for hh in range(4):
                                    tn.transpose(Ptc[:, hh * 128:(hh + 1) * 128],
                                                 scrF[:, t % 2, hh, c * 128:(c + 1) * 128], sidh[:])
                            PT = ph.add("tensor", ptr_, deps=[pv[("zrpn", t, 1)], pv[("pns", t, 1)],
                                                              pv.get(("pte", t, c - 1)),
                                                              pv.get(("qtre", t + 1))])
                            pv[("pt", t, c)] = PT
                            PTE = ph.add("vector", (lambda v, c=c: v.tensor_copy(pTb[:, c, :], Ptc[:, :])),
                                         deps=[PT, pv.get(("bt", t - 1))])
                            pv[("pte", t, c)] = PTE

                        def beta(tn, t=t):
                            for c in range(5):
                                tn.matmul(Pa[:, :], vsb[:, t + c, :], pTb[:, c, :],
                                          start=(c == 0), stop=(c == 4))
                        BT = ph.add("tensor", beta, deps=[pv[("pte", t, 4)], pv.get(("be", t - 1))])
                        pv[("bt", t)] = BT
                        BE = ph.add("vector", (lambda v, t=t: v.tensor_copy(
                            aT[:, :, t * 128:(t + 1) * 128], Pa.ap().rearrange("p (a b) -> p a b", a=4))),
                            deps=[BT])
                        pv[("be", t)] = BE

                    def mk_q(t):
                        def ldm(s, t=t):
                            DL(s, smtm[:, t % 2, :], mtm[t, :, :])
                            DL(s, scosq_s[:, t % 2, :, :], cosq[t * 128:(t + 1) * 128, :, :])
                            DL(s, ssinq_s[:, t % 2, :, :], sinq[t * 128:(t + 1) * 128, :, :])
                        Lm = ph.add("sync", ldm,
                                    deps=[pv.get(("tse", t - 2, 3)), pv.get(("rope", t - 2))], dmas=3)
                        pv[("lm", t)] = Lm

                        def pqm(tn, t=t, q=q):
                            for qc in range(4):
                                tn.matmul(Pq2[t % 2][:, :], qmidT[:, qc, t * 128:(t + 1) * 128],
                                          swquq[:, qc, :],
                                          start=(qc == 0), stop=(qc == 3))
                        PQ = ph.add("tensor", pqm, deps=[pv.get(("rope", t - 2)), pv[("w1c", 3)]])

                        def qsq(s, t=t):
                            for hh in range(4):
                                ii = s.activation(jnkA[:, hh * 128:(hh + 1) * 128], Pq2[t % 2][:, hh * 128:(hh + 1) * 128],
                                                  AF.Square, accum_out=tc("qs%d" % hh))
                            SC(s, ii)
                            for hh in range(4):
                                s.activation(tc("qs%d" % hh), tc("qs%d" % hh), AF.Sqrt, bias=scvals[:, 2:3])
                        pv[("qs", t)] = ph.add("scalar", qsq, deps=[PQ, pv.get(("rope", t - 1))])

                    def mk_rope(t):
                        def rope(v, t=t):
                            for hh in range(4):
                                ii = v.reciprocal(tc("qs%d" % hh), tc("qs%d" % hh))
                            SV(v, ii)
                            pqv = Pq2[t % 2].ap().rearrange("p (h d) -> p h d", h=4)
                            s0 = scr0[:, 0:512].rearrange("p (h d) -> p h d", h=4)
                            s2 = scr2[:, 0:512].rearrange("p (h d) -> p h d", h=4)
                            x1, x2 = pqv[:, :, 0:32], pqv[:, :, 32:64]
                            cs, sn = scosq_s[:, t % 2, :, :], ssinq_s[:, t % 2, :, :]
                            v.tensor_mul(s2[:, :, 0:32], x1, cs)
                            v.tensor_mul(s2[:, :, 32:64], x2, sn)
                            v.tensor_sub(s0[:, :, 0:32], s2[:, :, 0:32], s2[:, :, 32:64])
                            v.tensor_mul(s2[:, :, 0:32], x1, sn)
                            v.tensor_mul(s2[:, :, 32:64], x2, cs)
                            v.tensor_add(s0[:, :, 32:64], s2[:, :, 0:32], s2[:, :, 32:64])
                            for hh in range(4):
                                v.tensor_scalar_mul(scr1[:, hh * 128:hh * 128 + 64],
                                                    scr0[:, hh * 128:hh * 128 + 64], tc("qs%d" % hh))
                                v.tensor_scalar_mul(scr1[:, hh * 128 + 64:hh * 128 + 128],
                                                    Pq2[t % 2][:, hh * 128 + 64:hh * 128 + 128], tc("qs%d" % hh))
                            v.tensor_mul(qro[:], scr1[:, 0:512], sqwr[:])
                        pv[("rope", t)] = ph.add("vector", rope,
                                                 deps=[pv[("qs", t)], pv.get(("qtr", t - 1)), pv[("lm", t)]])

                    mk_q(0)
                    mk_rope(0)
                    for t in range(dtl):
                        Lm = pv[("lm", t)]
                        if t + 1 < dtl:
                            mk_q(t + 1)

                        def qtr(tn, t=t):
                            for hh in range(4):
                                tn.transpose(Ptc[:, hh * 128:(hh + 1) * 128],
                                             qro[:, hh * 128:(hh + 1) * 128], sidh[:])
                        QT = ph.add("tensor", qtr, deps=[pv[("rope", t)], pv.get(("pte", t - 2, 4))])
                        pv[("qtr", t)] = QT
                        QTE = ph.add("vector", (lambda v: v.tensor_copy(qnTt[:, :, :], Ptc.ap().rearrange("p (a b) -> p a b", a=4))),
                                     deps=[QT, pv.get(("tm", t - 1, 3))])
                        pv[("qtre", t)] = QTE

                        def mk_tm(hh):
                            def tmm(tn, t=t, hh=hh):
                                tn.matmul(Ptm[hh % 2][:, 0:512], qnTt[:, hh, :], kTs[:, t:t + 4, :], start=True, stop=True)
                                tn.matmul(Ptm[hh % 2][:, 512:640], qnTt[:, hh, :], kTs[:, t + 4, :], start=True, stop=True)
                            pv[("tm", t, hh)] = ph.add("tensor", tmm,
                                                       deps=[QTE, Lm,
                                                             pv.get(("tse", t, hh - 2)) if hh >= 2
                                                             else pv.get(("tse", t - 1, hh + 2))])

                        def mk_tse(hh):
                            TSE = ph.add("vector", (lambda v, t=t, hh=hh: v.tensor_add(
                                scrF[:, t % 2, hh, :], Ptm[hh % 2][:], smtm[:, t % 2, :])),
                                deps=[pv[("tm", t, hh)], Lm, pv.get(("pt", t - 2, 4))])
                            pv[("tse", t, hh)] = TSE
                        def mk_ex1(hh):
                            dep2 = pv.get(("pnv", t, 0)) if hh >= 2 else pv.get(("pnv", t - 1, 1))
                            EX1 = ph.add("scalar", (lambda s, hh=hh, t=t: s.activation(
                                scrP[:, hh % 2, :], scrF[:, t % 2, hh, :], AF.Exp,
                                accum_out=tiny[:, TC["dn0"] + hh:TC["dn0"] + hh + 1])),
                                deps=[pv[("tse", t, hh)], dep2])
                            pv[("ex1", t, hh)] = EX1

                        def mk_sb(hh):
                            hg = q * 4 + hh
                            pv[("sb", t, hh)] = ph.add("scalar", (lambda s, hh=hh, hg=hg: s.activation(
                                tiny[:, TC["z0"] + hh:TC["z0"] + hh + 1],
                                tiny[:, TC["dn0"] + hh:TC["dn0"] + hh + 1],
                                AF.Identity, bias=ssinkexp[:, hg:hg + 1])),
                                deps=[pv[("ex1", t, hh)], pv.get(("zrpn", t - 1, hh // 2))])

                        def mk_zrpn(pr):
                            def zr(v, pr=pr):
                                SV(v, v.reciprocal(nzh[:, 2 * pr:2 * pr + 2],
                                                   tiny[:, TC["z0"] + 2 * pr:TC["z0"] + 2 * pr + 2]))
                            ZR = ph.add("vector", zr, deps=[pv[("sb", t, 2 * pr + 1)],
                                                            pv.get(("pns", t - 1, pr))])
                            h0, h1 = 2 * pr, 2 * pr + 1
                            pv[("pns", t, pr)] = ph.add("scalar", (lambda s, h0=h0, t=t: s.activation(
                                scrF[:, t % 2, h0, :], scrP[:, h0 % 2, :], AF.Identity,
                                scale=nzh[:, h0:h0 + 1])),
                                deps=[ZR, pv.get(("pt", t - 2, 4))])
                            pv[("pnv", t, pr)] = ph.add("vector", (lambda v, h1=h1, t=t: v.tensor_scalar_mul(
                                scrF[:, t % 2, h1, :], scrP[:, h1 % 2, :], nzh[:, h1:h1 + 1])),
                                deps=[ZR, pv.get(("pt", t - 2, 4))])
                            pv[("zrpn", t, pr)] = pv[("pnv", t, pr)]
                        mk_tm(0)
                        mk_tm(1)
                        mk_tse(0)
                        mk_tm(2)
                        mk_tse(1)
                        mk_tm(3)
                        mk_tse(2)
                        mk_tse(3)
                        mk_ex1(0)
                        mk_ex1(1)
                        mk_sb(0)
                        mk_sb(1)
                        mk_zrpn(0)
                        mk_ex1(2)
                        mk_ex1(3)
                        mk_sb(2)
                        mk_sb(3)
                        mk_zrpn(1)
                        if t + 1 < dtl:
                            mk_rope(t + 1)
                        if t >= 1:
                            stage_b(t - 1)
                    stage_b(dtl - 1)

                    for oc in range(16 if dow else 0):
                        def owm(tn, oc=oc):
                            for i in range(4):
                                tn.matmul(Pq2[0][:, :], w1g[:, i, oc * 128:(oc + 1) * 128], aT[:, i, 0:512],
                                          start=(i == 0), stop=(i == 3))
                            for i in range(4):
                                tn.matmul(Pq2[1][:, :], w1g[:, i, oc * 128:(oc + 1) * 128], aT[:, i, 512:1024],
                                          start=(i == 0), stop=(i == 3))
                        OM = ph.add("tensor", owm, deps=[pv[("be", NT - 1)], pv[("w1c", 3)],
                                                         pv.get(("rope", NT - 1)),
                                                         pv.get(("owe", oc - 1))])

                        def owe(s, oc=oc):
                            s.copy(a8st[:, oc % 2, 0:512], Pq2[0][:, :])
                            SC(s, s.copy(a8st[:, oc % 2, 512:1024], Pq2[1][:, :]))
                            DA(s, a8d[oc, :, :], a8st[:, oc % 2, :])
                        pv[("owe", oc)] = ph.add("scalar", owe, deps=[OM, pv.get(("owe", oc - 2))], dmas=1)
                    if dow:
                        ph.add("scalar", (lambda s: None), deps=[pv[("owe", 15)]])
                    ph.emit(nc)

                with ExitStack() as pst:
                    Pw = [pst.enter_context(nc.psum_tensor(f"PwD{q}_{i}", [128, 512], F32)) for i in range(8)]
                    ph = Phase(ctx)
                    pv = {}
                    for nq in range(4):
                        for kc in range(16):
                            Lw = ph.add("sync", (lambda s, nq=nq, kc=kc, q=q: DL(
                                s, ld0[:, kc % 2, 0:512],
                                ow2[q * 2048 + kc * 128:q * 2048 + (kc + 1) * 128, nq * 512:(nq + 1) * 512])),
                                deps=[pv.get(("cst", nq, kc - 2)), pv.get(("cst", nq - 1, 14 + kc))],
                                dmas=1)
                            La = ph.add("gpsimd", (lambda g, kc=kc: DG(g, a8s[:, kc % 4, :], a8d[kc, :, :])),
                                        deps=[pv.get(("mm", nq, kc - 4)) if kc >= 4 else pv.get(("mm", nq - 1, 12 + kc))],
                                        dmas=1)
                            CS = ph.add("vector", (lambda v, kc=kc: v.tensor_copy(
                                ow2b[:, kc % 4, :], ld0[:, kc % 2, 0:512])),
                                deps=[Lw, pv.get(("mm", nq, kc - 4)) if kc >= 4 else pv.get(("mm", nq - 1, 12 + kc))])
                            pv[("cst", nq, kc)] = CS

                            def mms(tn, nq=nq, kc=kc):
                                for t in range(NT):
                                    tn.matmul(Pw[t][:, :], a8s[:, kc % 4, t * 128:(t + 1) * 128],
                                              ow2b[:, kc % 4, :], start=(kc == 0), stop=(kc == 15))
                            pv[("mm", nq, kc)] = ph.add("tensor", mms,
                                                        deps=[CS, La, pv.get(("ev", nq - 1, NT - 1))])
                            emit_stats(ph, 2)
                        for t in range(NT):
                            if q == 0:
                                EV = ph.add("scalar", (lambda s, nq=nq, t=t: s.copy(
                                    yac[:, t, nq * 512:(nq + 1) * 512], Pw[t][:, :])), deps=[pv[("mm", nq, 15)]])
                            else:
                                EV = ph.add("vector", (lambda v, nq=nq, t=t: v.tensor_add(
                                    yac[:, t, nq * 512:(nq + 1) * 512], Pw[t][:, :],
                                    yac[:, t, nq * 512:(nq + 1) * 512])), deps=[pv[("mm", nq, 15)]])
                            pv[("ev", nq, t)] = EV
                    emit_stats_flush(ph)
                    ph.emit(nc)
            mark("d_qnTt", lambda: qnTt[:])
            emit_s2()
            mark("d_pTb", lambda: pTb[:])
            mark("d_aT", lambda: aT[:])
            stB.close()
            mark("d_yac", lambda: yac[:])

        if upto >= 5:
            st1.close()
            stF = ExitStack()
            enF = stF.enter_context
            w1Tp = enF(nc.sbuf_tensor("w1Tp", [128, 3, 2048], BF16))
            w1l = enF(nc.sbuf_tensor("w1l", [128, 3, 2048], F32))
            h1o = enF(nc.sbuf_tensor("h1o", [128, 2, NTOK], FP16))
            xq1T = enF(nc.sbuf_tensor("xq1T", [128, 16, NTOK], BF16))
            with ExitStack() as pst:
                ph1 = [pst.enter_context(nc.psum_tensor(f"ph1_{i}", [128, 1024], F32)) for i in range(3)]
                ph = Phase(ctx)
                pv = {}
                xl = ph.add("sync", (lambda s: [DL(s, xq1T[:, :, t * 128:(t + 1) * 128], xq1d[t, :, :, :])
                                                for t in range(NT)] and None), dmas=NT)
                ph.add("sync", (lambda s: None), deps=[xl])
                def f1_prep(c):
                    Lw = ph.add("sync", (lambda s, c=c: DL(s, w1l[:, c % 3, 0:1024], w1p[:, c, 0:1024])),
                                deps=[pv.get(("trn", c - 3))], dmas=1)
                    Lw2 = ph.add("gpsimd", (lambda g, c=c: DG(g, w1l[:, c % 3, 1024:2048], w1p[:, c, 1024:2048])),
                                 deps=[pv.get(("trn", c - 3))], dmas=1)

                    def trn(v, c=c):
                        for hf in range(2):
                            sl = slice(hf * 1024, (hf + 1) * 1024)
                            v.tensor_scalar(scr0[:], w1l[:, c % 3, sl], tauc[:, 0:1], None, ALU.is_ge)
                            v.scalar_tensor_tensor(w1Tp[:, c % 3, sl], w1l[:, c % 3, sl], tauc[:, 1:2], scr0[:], ALU.is_ge, ALU.add)
                    TR = ph.add("vector", trn, deps=[Lw, Lw2, pv.get(("mm", c - 3))])
                    pv[("trn", c)] = TR

                    def mm(tn, c=c):
                        for kk in range(16):
                            for hf in range(2):
                                tn.matmul(ph1[c % 3][:, hf * 512:(hf + 1) * 512], w1Tp[:, c % 3, kk * 128:(kk + 1) * 128],
                                          xq1T[:, kk, hf * 512:(hf + 1) * 512],
                                          start=(kk == 0), stop=(kk == 15))
                    MM = ph.add("tensor", mm, deps=[TR, pv.get(("evu", c - 3))])
                    pv[("mm", c)] = MM

                def f1_consume(c):
                    def evu(v, c=c):
                        v.tensor_mul(scr1[:, 0:1024], ph1[c % 3][:], invrep[:])
                        v.tensor_sub(scr2[:, 0:1024], scr1[:, 0:1024], r2rep[:])
                    EV = ph.add("vector", evu, deps=[pv[("mm", c)], pv.get(("gel", c - 1))])
                    pv[("evu", c)] = EV
                    GE = ph.add("scalar", (lambda s, c=c: s.activation(h1o[:, c % 2, :], scr2[:, 0:1024], AF.Gelu)),
                                deps=[EV, pv.get(("h1w", c - 2)), pv.get(("va", c - 2))])
                    pv[("gel", c)] = GE

                    def va(v, c=c):
                        hc = h1o[:, c % 2, :]
                        if c == 0:
                            SV(v, v.tensor_scalar(absacc[:], hc, -1.0, None, ALU.mult))
                        else:
                            SV(v, v.scalar_tensor_tensor(absacc[:], hc, -1.0, absacc[:], ALU.mult, ALU.max))
                        v.tensor_tensor(absacc[:], absacc[:], hc, ALU.max)
                    pv[("va", c)] = ph.add("vector", va, deps=[GE])
                    pv[("h1w", c)] = ph.add("scalar", (lambda s, c=c: DA(s, h1gd[c, :, :], h1o[:, c % 2, :])),
                                            deps=[GE], dmas=1)
                for c in range(64):
                    f1_prep(c)
                    if c >= 1:
                        f1_consume(c - 1)
                f1_consume(63)
                ph.emit(nc)
            if upto >= 6:
                stF.close()

        if upto == 5:
            mark("d_xq1Ts", lambda: xq1T[:])
            mark("d_w1Tp", lambda: w1Tp[:])

        if upto >= 6:
            stF2 = ExitStack()
            enF2 = stF2.enter_context
            s2rep = enF2(nc.sbuf_tensor("s2rep", [128, NTOK], F32))
            with ExitStack() as pst:
                Pa = pst.enter_context(nc.psum_tensor("PaF", [128, 512], F32))
                prep2 = pst.enter_context(nc.psum_tensor("prepF", [128, 1024], F32))
                ph = Phase(ctx)
                T1 = ph.add("tensor", (lambda tn: [tn.transpose(Pa[:, i * 128:(i + 1) * 128],
                                                                absacc[:, i * 128:(i + 1) * 128], sidf[:])
                                                   for i in range(4)] and None))
                C1 = ph.add("scalar", (lambda s: s.copy(scr0[:, 0:512], Pa[:])), deps=[T1])
                T2 = ph.add("tensor", (lambda tn: [tn.transpose(Pa[:, i * 128:(i + 1) * 128],
                                                                absacc[:, (4 + i) * 128:(5 + i) * 128], sidf[:])
                                                   for i in range(4)] and None), deps=[C1])
                C2 = ph.add("scalar", (lambda s: s.copy(scr0[:, 512:1024], Pa[:])), deps=[T2])

                def red2(v):
                    for tt in range(NT):
                        ii = v.tensor_reduce(scr1[:, tt:tt + 1], scr0[:, tt * 128:(tt + 1) * 128], AX.X, ALU.max)
                    SV(v, ii)
                    v.tensor_scalar(scr1[:, 0:NT], scr1[:, 0:NT], 1e-5, None, ALU.max)
                def v3f(v):
                    SV(v, v.reciprocal(scr0[:, 0:NT], scr1[:, 0:NT]))
                    SV(v, v.tensor_scalar(s2c[:, 0:NT], scr0[:, 0:NT], 127.0, None, ALU.mult))
                    SV(v, v.tensor_scalar_mul(scr0[:, 0:NT], s2c[:, 0:NT], tauc[:, 5:6]))
                    v.reciprocal(inv2c[:, 0:NT], scr0[:, 0:NT])
                R2 = ph.add("vector", red2, deps=[C2])
                C4 = ph.add("vector", v3f, deps=[R2])
                C5 = C4
                for bb in range(2):
                    def trb2(tn, bb=bb):
                        for c in range(4):
                            tn.transpose(Pa[0:1, c * 128:(c + 1) * 128], s2c[:, bb * 4 + c:bb * 4 + c + 1], sidf[:])
                    T3 = ph.add("tensor", trb2, deps=[C5])
                    C5 = ph.add("scalar", (lambda s, bb=bb: s.copy(zrowf[0:1, bb * 512:(bb + 1) * 512], Pa[0:1, 0:512])),
                                deps=[T3])

                def rank3(tn):
                    for tt in range(NT):
                        tn.matmul(prep2[:, tt * 128:(tt + 1) * 128], sones1f[:],
                                  zrowf[0:1, tt * 128:(tt + 1) * 128], start=True, stop=True)
                T4 = ph.add("tensor", rank3, deps=[C5])
                ph.add("scalar", (lambda s: s.copy(s2rep[:], prep2[:])), deps=[T4])
                ph.emit(nc)

            xt = enF2(nc.sbuf_tensor("xt", [128, 4, NTOK], FP16))
            with ExitStack() as pst:
                pv2 = [pst.enter_context(nc.psum_tensor(f"pv2_{i}", [128, 512], F32)) for i in range(8)]
                ph = Phase(ctx)
                pv = {}
                def f2b_prep(c):
                    Lh = ph.add("gpsimd", (lambda g, c=c: DG(g, xt[:, c % 4, :], h1gd[c, :, :])),
                                deps=[pv.get(("xqw", c - 4))], dmas=1)

                    def qmul(v, c=c):
                        v.tensor_mul(scr1[:, (c % 2) * 1024:(c % 2) * 1024 + 1024], xt[:, c % 4, :], s2rep[:])
                    QM = ph.add("vector", qmul, deps=[Lh, pv.get(("qu", c - 2))])
                    QC_ = ph.add("scalar", (lambda s, c=c: s.activation(
                        scri[:, c % 2, :], scr1[:, (c % 2) * 1024:(c % 2) * 1024 + 1024],
                        AF.Identity, bias=scvals[:, 4:5])),
                        deps=[QM, pv.get(("qu", c - 2))])
                    pv[("qc", c)] = QC_

                def f2b_consume(c):
                    def quant(v, c=c):
                        sl1 = scr1[:, (c % 2) * 1024:(c % 2) * 1024 + 1024]
                        v.scalar_tensor_tensor(scr0[:, 0:1024], sl1, RBIAS, scri[:, c % 2, :], ALU.add, ALU.subtract)
                        v.tensor_scalar(scr2[:, 0:1024], scr0[:, 0:1024], 1.0, None, ALU.is_ge)
                        v.scalar_tensor_tensor(scr2[:, 0:1024], scr0[:, 0:1024], 0.0, scr2[:, 0:1024], ALU.is_ge, ALU.add)
                        v.scalar_tensor_tensor(xt[:, c % 4, :], scr2[:, 0:1024], -257.0, scri[:, c % 2, :], ALU.add, ALU.add)
                    QU = ph.add("vector", quant, deps=[pv[("qc", c)]])
                    pv[("qu", c)] = QU

                    def v2mm(tn, c=c):
                        for t in range(NT):
                            tn.matmul(pv2[t][:, 0:1], xt[:, c % 4, t * 128:(t + 1) * 128], sonescolh[:],
                                      start=(c == 0), stop=(c == 63))
                    VM = ph.add("tensor", v2mm, deps=[QU])
                    pv[("xqw", c)] = ph.add("scalar", (lambda s, c=c: DA(s, xq2d[c, :, :], xt[:, c % 4, :])),
                                            deps=[VM], dmas=1)
                for c in range(64):
                    f2b_prep(c)
                    if c >= 1:
                        f2b_consume(c - 1)
                f2b_consume(63)

                def vev(s):
                    for t in range(NT):
                        s.copy(v2s[:, t:t + 1], pv2[t][:, 0:1])
                EVV = ph.add("scalar", vev, deps=[pv[("xqw", 63)]])
                def vb2(v):
                    SV(v, v.tensor_mul(scr0[:, 0:NT], v2s[:, 0:NT], inv2c[:, 0:NT]))
                    v.tensor_scalar(bias2c[:, 0:NT], scr0[:, 0:NT], -1.0, None, ALU.mult)
                ph.add("vector", vb2, deps=[EVV])
                ph.emit(nc)

            w2s = enF2(nc.sbuf_tensor("w2s", [128, 2, 512], FP16))
            xs = enF2(nc.sbuf_tensor("xs", [128, 2, 512], F32))
            xqs = enF2(nc.sbuf_tensor("xqs", [128, 2, NTOK], FP16))
            ob = enF2(nc.sbuf_tensor("ob", [128, 2, 512], F32))
            with ExitStack() as pst:
                Pw = [pst.enter_context(nc.psum_tensor(f"PwF{i}", [128, 512], F32)) for i in range(NT)]
                for nq in range(4):
                    ph = Phase(ctx)
                    pv = {}
                    for kc in range(64):
                        Lw = ph.add("sync", (lambda s, nq=nq, kc=kc: DL(
                            s, ld0[:, kc % 2, 0:512], w2p[:, nq, kc, :])),
                            deps=[pv.get(("t2", nq, kc - 2)) if kc >= 2 else pv.get(("t2", nq - 1, 62 + kc))],
                            dmas=1)

                        def trn2(v, kc=kc):
                            v.tensor_scalar(scr0[:, 0:512], ld0[:, kc % 2, 0:512], tauc[:, 2:3], None, ALU.is_ge)
                            v.scalar_tensor_tensor(w2s[:, kc % 2, :], ld0[:, kc % 2, 0:512], tauc[:, 3:4],
                                                   scr0[:, 0:512], ALU.is_ge, ALU.add)
                        TN = ph.add("vector", trn2,
                                    deps=[Lw, pv.get(("mm", nq, kc - 2)) if kc >= 2 else pv.get(("mm", nq - 1, 62 + kc))])
                        pv[("t2", nq, kc)] = TN

                        Lx = ph.add("gpsimd", (lambda g, kc=kc: DG(g, xqs[:, kc % 2, :], xq2d[kc, :, :])),
                                    deps=[pv.get(("mm", nq, kc - 2)) if kc >= 2 else pv.get(("mm", nq - 1, 62 + kc))],
                                    dmas=1)

                        def mm2(tn, kc=kc):
                            for t in range(NT):
                                tn.matmul(Pw[t][:, :], xqs[:, kc % 2, t * 128:(t + 1) * 128],
                                          w2s[:, kc % 2, :], start=(kc == 0), stop=(kc == 63))
                        pv[("mm", nq, kc)] = ph.add("tensor", mm2,
                                                    deps=[TN, Lx, pv.get(("fin", nq * NT - 1))])
                    for t in range(NT):
                        u = nq * NT + t
                        Lxr = ph.add("gpsimd", (lambda g, t=t, nq=nq, u=u: DG(
                            g, xs[:, u % 2, :], hkv[512 + t * 128:512 + (t + 1) * 128, nq * 512:(nq + 1) * 512])),
                            deps=[pv.get(("fin", u - 2))], dmas=1)
                        CE = ph.add("scalar", (lambda s, t=t, nq=nq: s.activation(
                            scr2[:, 0:512], Pw[t][:, :], AF.Identity, bias=bias2c[:, t:t + 1],
                            scale=inv2c[:, t:t + 1])), deps=[pv[("mm", nq, 63)], pv.get(("fin", u - 1))])

                        def fin(v, t=t, nq=nq, u=u):
                            v.tensor_add(scr1[:, 0:512], scr2[:, 0:512], xs[:, u % 2, :])
                            v.tensor_add(ob[:, u % 2, :], scr1[:, 0:512], yac[:, t, nq * 512:(nq + 1) * 512])
                        FN = ph.add("vector", fin, deps=[CE, Lxr, pv.get(("ot", u - 2))])
                        pv[("fin", u)] = FN
                        OT = ph.add("sync", (lambda s, t=t, nq=nq, u=u: DO(
                            s, y[t * 128:(t + 1) * 128, nq * 512:(nq + 1) * 512], ob[:, u % 2, :])), deps=[FN])
                        pv[("ot", u)] = OT
                    ph.emit(nc)
            stF2.close()

        ph = Phase(ctx)

        def fino(s):
            for name, apf in dump_specs:
                DO(s, dbg_t[name].ap(), apf())
            if upto < 6:
                for t in range(NT):
                    DO(s, y[t * 128:(t + 1) * 128, 0:1024], scr1[:, 0:1024])
                    DO(s, y[t * 128:(t + 1) * 128, 1024:2048], scr1[:, 1024:2048])
            for q in ("sync", "gpsimd", "scalar"):
                n = dctr[q].v
                for r in range(NPOOL):
                    cnt = n // NPOOL + (1 if n % NPOOL > r else 0)
                    if cnt:
                        s.wait_ge(dsem[q][r], 16 * cnt)
        ph.add("sync", fino)
        ph.emit(nc)
        if upto < 3:
            stA.close()
        if upto < 5:
            st1.close()
        if upto == 5:
            stF.close()
    return nc, dbg_t



_W1P = [None]
_W2P = [None]


def prep_weight_perms(inputs):
    w1 = np.asarray(inputs["L0_ffn_w1"], np.float32)
    w2 = np.asarray(inputs["L0_ffn_w2"], np.float32)
    _W1P[0] = np.ascontiguousarray(
        w1.reshape(64, 128, 16, 128).transpose(3, 0, 2, 1).reshape(128, 64, 2048))
    _W2P[0] = np.ascontiguousarray(
        w2.reshape(4, 512, 64, 128).transpose(3, 0, 2, 1))


def prep_core_inputs(core, hf, inputs):
    if _W1P[0] is None:
        prep_weight_perms(inputs)
    half = core % 2
    seq = core // 2
    base = seq * T_SEQ + half * NTOK
    if half == 0:
        hkv = np.concatenate([np.zeros((512, HID), np.float32),
                              hf[base:base + NTOK]], axis=0)
    else:
        hkv = np.ascontiguousarray(hf[base - 512:base + NTOK])

    pos = np.maximum(np.arange(NKV, dtype=np.int64) + (half * NTOK - 512), 0).astype(np.float64)
    inv = 1.0 / (10000.0 ** (np.arange(0, 64, 2, dtype=np.float64) / 64.0))
    ang = pos[:, None] * inv[None, :]
    cosk = np.cos(ang).astype(np.float32)
    sinkt = np.sin(ang).astype(np.float32)
    cosq = np.ascontiguousarray(np.tile(cosk[512:, None, :], (1, 4, 1)).astype(np.float32))
    sinq = np.ascontiguousarray(np.tile(sinkt[512:, None, :], (1, 4, 1)).astype(np.float32))

    j = np.arange(NT)[:, None, None, None]
    r = np.arange(128)[None, :, None, None]
    c = np.arange(5)[None, None, :, None]
    kk = np.arange(128)[None, None, None, :]
    kpos = half * NTOK - 512 + (j + c) * 128 + kk
    qp = half * NTOK + j * 128 + r
    valid = ((qp - kpos) >= 0) & ((qp - kpos) < WIN) & (kpos >= 0)
    add = np.where(valid, np.float32(0), np.float32(-60000.0)).astype(BFD)
    mtm = np.ascontiguousarray(add.reshape(NT, 128, 640))
    mdt = np.ascontiguousarray(np.transpose(add, (0, 3, 2, 1)))

    return {
        "hkv": hkv,
        "wqc": inputs["L0_attn_wq_comp"], "wqu": inputs["L0_attn_wq_up"],
        "wk": inputs["L0_attn_wk"], "wv": inputs["L0_attn_wv"],
        "ow1": inputs["L0_attn_out_w1"], "ow2": inputs["L0_attn_out_w2"],
        "w1p": _W1P[0], "w2p": _W2P[0],
        "qwr": np.ascontiguousarray(np.tile(np.asarray(inputs["L0_attn_q_norm_w"], np.float32)[None, :], (128, 4))),
        "kwr": np.ascontiguousarray(np.tile(np.asarray(inputs["L0_attn_k_norm_w"], np.float32)[None, :], (128, 1))),
        "sinkr": np.ascontiguousarray(np.tile(np.asarray(inputs["L0_attn_sink_logit"], np.float32)[None, :], (128, 1))),
        "nsinkr": np.ascontiguousarray(np.tile(-np.asarray(inputs["L0_attn_sink_logit"], np.float32)[None, :], (128, 1))),
        "cosq": cosq, "sinq": sinq, "cosk": cosk, "sinkt": sinkt,
        "mtm": mtm, "mdt": mdt,
        "idb": np.eye(128, dtype=BFD),
        "idf": np.eye(128, dtype=np.float32),
        "idh": np.eye(128, dtype=np.float16),
        "ones1f": np.ones((1, 128), np.float32),
        "onescolh": np.ones((128, 1), np.float16),
        "cvals": np.tile(np.array([[1e-5, 1e-6, 128e-6, 0.0, 256.5]], np.float32), (128, 1)),
        "onescolf": np.ones((128, 1), np.float32),
    }


def kernel(**inputs):
    hf = np.ascontiguousarray(np.asarray(inputs["L0_attn_h"], np.float32).reshape(4 * T_SEQ, HID))
    prep_weight_perms(inputs)
    nc, _ = build()
    in_maps = [prep_core_inputs(cc, hf, inputs) for cc in range(8)]
    res = run_bass_kernel_spmd(nc, in_maps, core_ids=list(range(8)))
    out = np.concatenate([np.asarray(r["y"], np.float32) for r in res.results], axis=0)
    return out.reshape(4, T_SEQ, HID)
